# version 06
# baseline (speedup 1.0000x reference)
import math
from contextlib import ExitStack

import numpy as np

import concourse.bass as bass
import concourse.mybir as mybir

F32 = mybir.dt.float32
BF16 = mybir.dt.bfloat16
I32 = mybir.dt.int32
AF = mybir.ActivationFunctionType
ALU = mybir.AluOpType
AX = mybir.AxisListType

D = 2048
DC = 16
S = 4096
T = 1024
NH = 16
NKV = 4
HD = 128
II = 8192
ICN = 64
NTC = 8
RSQ = 1.0 / math.sqrt(HD)
TAU = 2.0 * math.pi
EPS = 1e-5

ENGINES = ("tensor", "vector", "scalar", "gpsimd", "sync")


class Chan:
    def __init__(self, ctx, nc, name, step=1):
        self.sem = ctx.enter_context(nc.semaphore(name))
        self.n = 0
        self.step = step


class Rec:

    def __init__(self, nc):
        self.nc = nc
        self.ops = {e: [] for e in ENGINES}

    def emit(self, eng, fn, inc=None, k=None):
        if inc is not None:
            kk = inc.step if k is None else k
            inc.n += kk
            sem = inc.sem
            self.ops[eng].append(lambda e, fn=fn, sem=sem, kk=kk: fn(e).then_inc(sem, kk))
        else:
            self.ops[eng].append(fn)

    def wait(self, eng, chan, at=None):
        tgt = chan.n if at is None else at
        sem = chan.sem
        if tgt > 0:
            self.ops[eng].append(lambda e, sem=sem, tgt=tgt: e.wait_ge(sem, tgt))

    def replay(self):
        nc = self.nc
        ops = self.ops
        self.ops = {e: [] for e in ENGINES}
        with nc.Block() as blk:
            if ops["tensor"]:
                @blk.tensor
                def _(t):
                    for fn in ops["tensor"]:
                        fn(t)
            if ops["vector"]:
                @blk.vector
                def _(v):
                    for fn in ops["vector"]:
                        fn(v)
            if ops["scalar"]:
                @blk.scalar
                def _(s):
                    for fn in ops["scalar"]:
                        fn(s)
            if ops["gpsimd"]:
                @blk.gpsimd
                def _(g):
                    for fn in ops["gpsimd"]:
                        fn(g)
            if ops["sync"]:
                @blk.sync
                def _(sy):
                    for fn in ops["sync"]:
                        fn(sy)


def build_nc(debug=()):
    nc = bass.Bass("TRN2", target_bir_lowering=False)
    dbg = set(debug)

    x_slab = nc.dram_tensor("x_slab", [S, D], F32, kind="ExternalInput").ap()
    x_own = nc.dram_tensor("x_own", [T, D], F32, kind="ExternalInput").ap()
    pos_q = nc.dram_tensor("pos_q", [64, T], F32, kind="ExternalInput").ap()
    wq_t = nc.dram_tensor("wq_t", [D, D], F32, kind="ExternalInput").ap()
    wk_t = nc.dram_tensor("wk_t", [D, NKV * HD], F32, kind="ExternalInput").ap()
    wv_t = nc.dram_tensor("wv_t", [D, NKV * HD], F32, kind="ExternalInput").ap()
    wo_t = nc.dram_tensor("wo_t", [D, D], F32, kind="ExternalInput").ap()
    wg_t4 = nc.dram_tensor("wg_t4", [ICN, 128, D], I32, kind="ExternalInput").ap()
    wu_t4 = nc.dram_tensor("wu_t4", [ICN, 128, D], I32, kind="ExternalInput").ap()
    wd_t = nc.dram_tensor("wd_t", [ICN, 128, D], I32, kind="ExternalInput").ap()
    gs_t = nc.dram_tensor("gs_t", [128, ICN], F32, kind="ExternalInput").ap()
    us_t = nc.dram_tensor("us_t", [128, ICN], F32, kind="ExternalInput").ap()
    ds_b = nc.dram_tensor("ds_b", [128, D], F32, kind="ExternalInput").ap()
    out = nc.dram_tensor("out", [T, D], F32, kind="ExternalOutput").ap()
    x1_scr = nc.dram_tensor("x1_scr", [T, D], F32, kind="Internal").ap()
    y_scr = nc.dram_tensor("y_scr", [T, D], F32, kind="Internal").ap()
    oT_scr = nc.dram_tensor("oT_scr", [NH, 128, T], BF16, kind="Internal").ap()

    dbg_ap = {}

    def dbg_tensor(name, shape, dtype=F32):
        if name in dbg:
            dbg_ap[name] = nc.dram_tensor("dbg_" + name, shape, dtype,
                                          kind="ExternalOutput").ap()
        return dbg_ap.get(name)

    ctx = ExitStack()
    with ctx:
        sbp = lambda name, shape, dt: ctx.enter_context(nc.sbuf_tensor(name, shape, dt))
        ident = sbp("ident", [128, 128], BF16)
        ones_col = sbp("ones_col", [128, 1], BF16)
        onesf_row = sbp("onesf_row", [1, 128], F32)
        c128_2 = sbp("c128_2", [2, 128], BF16)
        ones_bfr = sbp("ones_bfr", [1, 128], BF16)
        eps_t = sbp("eps_t", [128, 1], F32)
        shift_sb = sbp("shift_sb", [128, NH], F32)
        m2q16 = sbp("m2q16", [1, NH], F32)
        m2k4 = sbp("m2k4", [1, NKV], F32)
        rr_sb = sbp("rr_sb", [1, T], F32)
        hs128 = sbp("hs128", [128, NTC], F32)
        mean_t2 = sbp("mean_t", [128, 2], F32)
        negm_t2 = sbp("negm_t", [128, 2], F32)
        var_t2 = sbp("var_t", [128, 2], F32)
        istd_t2 = sbp("istd_t", [128, 2], F32)
        t11 = sbp("t11", [1, 16], F32)
        dbg_ln0 = sbp("dbgln0_sb", [128, 8], F32)

        pe = Chan(ctx, nc, "pe")
        ve = Chan(ctx, nc, "ve")
        se = Chan(ctx, nc, "se")
        gp = Chan(ctx, nc, "gp")
        cin = [Chan(ctx, nc, f"cin{i}", 16) for i in range(3)]
        cw = Chan(ctx, nc, "cw", 16)
        cwg = [Chan(ctx, nc, f"cwg{i}", 16) for i in range(2)]
        cwu = [Chan(ctx, nc, f"cwu{i}", 16) for i in range(2)]
        cwd = [Chan(ctx, nc, f"cwd{i}", 16) for i in range(2)]
        cst = Chan(ctx, nc, "cst", 16)
        cfy = [Chan(ctx, nc, f"cfy{i}", 16) for i in range(2)]
        cfx = [Chan(ctx, nc, f"cfx{i}", 16) for i in range(2)]
        cot = [Chan(ctx, nc, f"cot{i}", 16) for i in range(2)]

        R = Rec(nc)

        def emit_ln_stats(src_ap, bix=0, sq_dst=None):
            mean_t = mean_t2[:, bix:bix + 1]
            negm_t = negm_t2[:, bix:bix + 1]
            var_t = var_t2[:, bix:bix + 1]
            istd_t = istd_t2[:, bix:bix + 1]
            R.emit("vector", lambda v, a=src_ap: v.reduce_sum(mean_t, a, axis=AX.X),
                   inc=ve)
            R.wait("vector", ve)
            R.emit("vector", lambda v: v.tensor_scalar(mean_t, mean_t, 1.0 / D,
                                                       None, ALU.mult), inc=ve)
            R.wait("vector", ve)
            R.emit("vector", lambda v: v.tensor_scalar(negm_t, mean_t, -1.0,
                                                       None, ALU.mult), inc=ve)
            R.wait("scalar", ve)
            R.emit("scalar", lambda s, a=src_ap, d=sq_dst: s.activation(
                d, a, AF.Square, bias=negm_t, accum_out=var_t))
            R.emit("scalar", lambda s: s.activation(var_t, var_t, AF.Sqrt,
                                                    bias=eps_t[:], scale=1.0 / D), inc=se)
            R.wait("vector", se)
            R.emit("vector", lambda v: v.reciprocal(istd_t, var_t), inc=ve)

        def emit_ln_apply(dst_ap, src_ap, inc=None, bix=0):
            mean_t = mean_t2[:, bix:bix + 1]
            istd_t = istd_t2[:, bix:bix + 1]
            R.wait("vector", ve)
            R.emit("vector", lambda v, d=dst_ap, a=src_ap: v.tensor_scalar(
                d, a, mean_t, istd_t, ALU.subtract, ALU.mult), inc=inc)

        sc_ropes = ExitStack()
        tabC_k = sc_ropes.enter_context(nc.sbuf_tensor("tabC_k", [128, S], BF16))
        tabS2_k = sc_ropes.enter_context(nc.sbuf_tensor("tabS2_k", [128, S], BF16))

        sc_qT = ExitStack()
        qT_sb = sc_qT.enter_context(nc.sbuf_tensor("qT_sb", [128, NH * T], BF16))
        sc_ropes_q = ExitStack()
        tabC_q = sc_ropes_q.enter_context(nc.sbuf_tensor("tabC_q", [128, T], BF16))
        tabS2_q = sc_ropes_q.enter_context(nc.sbuf_tensor("tabS2_q", [128, T], BF16))

        sc_set = ExitStack()
        iot_id = sc_set.enter_context(nc.sbuf_tensor("iot_id", [128, 128], I32))
        posk_i = sc_set.enter_context(nc.sbuf_tensor("posk_i", [64, S], I32))
        ivf_i = sc_set.enter_context(nc.sbuf_tensor("ivf_i", [64, 1], I32))
        ivf = sc_set.enter_context(nc.sbuf_tensor("ivf", [64, 1], F32))
        frq = sc_set.enter_context(nc.sbuf_tensor("frq", [64, S], F32))
        rrd = sc_set.enter_context(nc.sbuf_tensor("rrd", [64, S], F32))
        msk = sc_set.enter_context(nc.sbuf_tensor("msk", [64, S], F32))
        nni = sc_set.enter_context(nc.sbuf_tensor("nni", [64, S], I32))
        posq_f = sc_set.enter_context(nc.sbuf_tensor("posq_f", [64, T], F32))

        R.emit("sync", lambda sy: sy.dma_start(out=posq_f[:], in_=pos_q), inc=cw)
        posq_loaded = cw.n
        R.emit("gpsimd", lambda g: g.iota(iot_id[:], pattern=[[1, 128]], base=0,
                                          channel_multiplier=-1))
        R.emit("gpsimd", lambda g: g.iota(posk_i[:], pattern=[[1, S]], base=0,
                                          channel_multiplier=0))
        R.emit("gpsimd", lambda g: g.iota(ivf_i[:], pattern=[[1, 1]], base=0,
                                          channel_multiplier=1), inc=gp)
        R.emit("vector", lambda v: v.memset(ones_col[:], 1.0))
        R.emit("vector", lambda v: v.memset(onesf_row[:], 1.0))
        R.emit("vector", lambda v: v.memset(c128_2[0:2, :], -128.0))
        R.emit("vector", lambda v: v.memset(ones_bfr[0:1, :], 1.0))
        R.emit("vector", lambda v: v.memset(eps_t[:], EPS))
        R.wait("vector", gp)
        R.emit("vector", lambda v: v.tensor_scalar(ident[:], iot_id[:], 0, None,
                                                   ALU.is_equal))
        R.emit("vector", lambda v: v.tensor_copy(ivf[:], ivf_i[:]), inc=ve)
        R.wait("scalar", ve)
        R.emit("scalar", lambda s: s.activation(ivf[:], ivf[:], AF.Exp,
                                                scale=-2.0 * math.log(10000.0) / HD),
               inc=se)
        R.wait("vector", se)

        def emit_sin(src_ap, dst_ap, width, phase):
            if phase != 0.0:
                R.emit("vector", lambda v, a=src_ap: v.tensor_scalar(
                    rrd[0:64, 0:width], a, phase, None, ALU.add))
                a0 = rrd[0:64, 0:width]
            else:
                a0 = src_ap
            R.emit("vector", lambda v, a=a0: v.tensor_scalar(
                msk[0:64, 0:width], a, 1.0 / TAU, None, ALU.mult))
            R.emit("vector", lambda v: v.tensor_copy(nni[0:64, 0:width], msk[0:64, 0:width]))
            R.emit("vector", lambda v: v.tensor_copy(msk[0:64, 0:width], nni[0:64, 0:width]))
            R.emit("vector", lambda v, a=a0: v.scalar_tensor_tensor(
                rrd[0:64, 0:width], msk[0:64, 0:width], -TAU, a, ALU.mult, ALU.add))
            R.emit("vector", lambda v: v.tensor_scalar(
                msk[0:64, 0:width], rrd[0:64, 0:width], math.pi, None, ALU.is_gt))
            R.emit("vector", lambda v: v.scalar_tensor_tensor(
                rrd[0:64, 0:width], msk[0:64, 0:width], -TAU, rrd[0:64, 0:width],
                ALU.mult, ALU.add))
            R.emit("vector", lambda v: v.tensor_scalar(
                msk[0:64, 0:width], rrd[0:64, 0:width], -math.pi, None, ALU.is_lt))
            R.emit("vector", lambda v: v.scalar_tensor_tensor(
                rrd[0:64, 0:width], msk[0:64, 0:width], TAU, rrd[0:64, 0:width],
                ALU.mult, ALU.add), inc=ve)
            R.wait("scalar", ve)
            R.emit("scalar", lambda s, d=dst_ap: s.activation(d, rrd[0:64, 0:width],
                                                              AF.Sin), inc=se)
            R.wait("vector", se)

        R.emit("vector", lambda v: v.tensor_scalar(frq[0:64, :], posk_i[:], ivf[:],
                                                   None, ALU.mult))
        emit_sin(frq[0:64, :], tabS2_k[64:128, :], S, 0.0)
        R.emit("vector", lambda v: v.tensor_scalar(tabS2_k[0:64, :], tabS2_k[64:128, :],
                                                   -1.0, None, ALU.mult))
        emit_sin(frq[0:64, :], tabC_k[0:64, :], S, math.pi / 2)
        R.emit("vector", lambda v: v.tensor_copy(tabC_k[64:128, :], tabC_k[0:64, :]))
        R.wait("vector", cw, at=posq_loaded)
        R.emit("vector", lambda v: v.tensor_scalar(frq[0:64, 0:T], posq_f[:], ivf[:],
                                                   None, ALU.mult))
        emit_sin(frq[0:64, 0:T], tabS2_q[64:128, :], T, 0.0)
        R.emit("vector", lambda v: v.tensor_scalar(tabS2_q[0:64, :], tabS2_q[64:128, :],
                                                   -1.0, None, ALU.mult))
        emit_sin(frq[0:64, 0:T], tabC_q[0:64, :], T, math.pi / 2)
        R.emit("vector", lambda v: v.tensor_copy(tabC_q[64:128, :], tabC_q[0:64, :]),
               inc=ve)
        R.replay()
        sc_set.close()


        def emit_rope(dst_ap, ps_ap, tabC, tabS2, col0, width, scr, inc=None):
            xf = scr[:, 0:width]
            xsw = scr[:, width:2 * width]
            tc_ = tabC[:, col0:col0 + width]
            ts_ = tabS2[:, col0:col0 + width]
            R.emit("vector", lambda v: v.tensor_copy(xf, ps_ap[:, 0:width]))
            R.emit("vector", lambda v: v.tensor_copy(xsw[0:64, :], ps_ap[64:128, 0:width]))
            R.emit("vector", lambda v: v.tensor_copy(xsw[64:128, :], ps_ap[0:64, 0:width]))
            R.emit("vector", lambda v: v.tensor_tensor(xsw, xsw, ts_, ALU.mult))
            R.emit("vector", lambda v: v.tensor_tensor(xf, xf, tc_, ALU.mult))
            R.emit("vector", lambda v, d=dst_ap: v.tensor_add(d, xf, xsw), inc=inc)

        sc_a = ExitStack()
        wq_sb = sc_a.enter_context(nc.sbuf_tensor("wq_sb", [128, DC * D], BF16))
        xnT_own = sc_a.enter_context(nc.sbuf_tensor("xnT_own", [128, DC * T], BF16))
        x_in = sc_a.enter_context(nc.sbuf_tensor("x_in", [128, 3 * D], BF16))
        xn_bf = sc_a.enter_context(nc.sbuf_tensor("xn_bf", [128, 3 * D], BF16))
        rope_scr = sc_a.enter_context(nc.sbuf_tensor("rope_scr", [128, 1024], F32))
        ps_mm0 = sc_a.enter_context(nc.psum_tensor("ps_mm0", [128, 512], F32))
        ps_mm1 = sc_a.enter_context(nc.psum_tensor("ps_mm1", [128, 512], F32))
        ps_tr0 = sc_a.enter_context(nc.psum_tensor("ps_tr0", [128, 1024], BF16))
        ps_tr1 = sc_a.enter_context(nc.psum_tensor("ps_tr1", [128, 1024], BF16))
        ps_m2 = sc_a.enter_context(nc.psum_tensor("ps_m2", [128, 512], F32))
        ps_mm = [ps_mm0, ps_mm1]
        ps_tr = [ps_tr0, ps_tr1]

        for b in range(4):
            src = wq_t[b * 512:(b + 1) * 512, :].rearrange("(c p) q -> p c q", p=128)
            R.emit("gpsimd", lambda g, s=src, b=b: g.dma_start(
                out=wq_sb[:, b * 4 * D:(b + 1) * 4 * D].rearrange("p (c q) -> p c q", c=4),
                in_=s), inc=cw)
        wq_loaded = cw.n

        evac_se = []
        a0_ln_ve = []
        a0_ln_se = []
        tr_pe_tile = []

        def a0_emit_ln(tc):
            buf = tc % 3
            if tc >= 3:
                R.wait("gpsimd", ve, at=a0_ln_ve[tc - 3])
                R.wait("gpsimd", se, at=a0_ln_se[tc - 3])
            R.emit("gpsimd", lambda g, tc=tc, buf=buf: g.dma_start(
                out=x_in[:, buf * D:(buf + 1) * D],
                in_=x_own[tc * 128:(tc + 1) * 128, :]), inc=cin[buf])
            R.wait("vector", cin[buf])
            src = x_in[:, buf * D:(buf + 1) * D]
            if tc >= 3:
                R.wait("vector", pe, at=tr_pe_tile[tc - 3])
                R.wait("scalar", pe, at=tr_pe_tile[tc - 3])
            emit_ln_stats(src, bix=tc % 2, sq_dst=xn_bf[:, buf * D:(buf + 1) * D])
            a0_ln_se.append(se.n)
            emit_ln_apply(xn_bf[:, buf * D:(buf + 1) * D], src, inc=ve, bix=tc % 2)
            a0_ln_ve.append(ve.n)

        def a0_emit_tr(tc):
            buf = tc % 3
            for c in range(DC):
                gidx = tc * DC + c
                R.wait("tensor", ve, at=a0_ln_ve[tc])
                if gidx >= 2:
                    R.wait("tensor", se, at=evac_se[gidx - 2])
                R.emit("tensor", lambda t, buf=buf, c=c: t.transpose(
                    ps_tr[c % 2][:, 0:128],
                    xn_bf[:, buf * D + c * 128:buf * D + (c + 1) * 128],
                    ident[:]), inc=pe)
                R.wait("scalar", pe)
                R.emit("scalar", lambda s, c=c, tc=tc: s.copy(
                    xnT_own[:, c * T + tc * 128:c * T + (tc + 1) * 128],
                    ps_tr[c % 2][:, 0:128]), inc=se)
                evac_se.append(se.n)
            tr_pe_tile.append(pe.n)

        for tc in range(NTC):
            a0_emit_ln(tc)
            if tc >= 1:
                a0_emit_tr(tc - 1)
        a0_emit_tr(NTC - 1)
        a0_done_se = se.n

        if dbg_tensor("xnT_own", [128, DC * T], BF16) is not None:
            R.wait("sync", se, at=a0_done_se)
            R.emit("sync", lambda sy: sy.dma_start(out=dbg_ap["xnT_own"],
                                                   in_=xnT_own[:]), inc=cst)

        q_ve_done = []
        for h in range(NH):
            for qt in range(2):
                idx = h * 2 + qt
                b = idx % 2
                if idx == 0:
                    R.wait("tensor", cw, at=wq_loaded)
                    R.wait("tensor", se, at=a0_done_se)
                if idx >= 2:
                    R.wait("tensor", ve, at=q_ve_done[idx - 2])
                for c in range(DC):
                    R.emit("tensor", lambda t, h=h, qt=qt, c=c, b=b: t.matmul(
                        ps_mm[b][:],
                        wq_sb[:, c * D + h * 128:c * D + (h + 1) * 128],
                        xnT_own[:, c * T + qt * 512:c * T + (qt + 1) * 512],
                        start=(c == 0), stop=(c == DC - 1)),
                        inc=pe if c == DC - 1 else None)
                R.wait("vector", pe)
                emit_rope(qT_sb[:, h * T + qt * 512:h * T + (qt + 1) * 512],
                          ps_mm[b][:], tabC_q, tabS2_q, qt * 512, 512, rope_scr, inc=ve)
                q_ve_done.append(ve.n)
        R.wait("scalar", ve)
        m2_pe = 0
        for h in range(NH):
            if h > 0:
                R.wait("scalar", pe, at=m2_pe)
            R.emit("scalar", lambda s, h=h: s.activation(
                xn_bf[:, 0:1024], qT_sb[:, h * T:(h + 1) * T], AF.Square), inc=se)
            R.wait("tensor", se)
            for half in range(2):
                if h > 0 or half > 0:
                    R.wait("tensor", ve)
                R.emit("tensor", lambda t, half=half: t.matmul(
                    ps_m2[0:1, 0:512], ones_col[:],
                    xn_bf[:, half * 512:(half + 1) * 512],
                    start=True, stop=True), inc=pe)
                m2_pe = pe.n
                R.wait("vector", pe)
                R.emit("vector", lambda v: v.reduce_max(
                    t11[0:1, 0:1], ps_m2[0:1, 0:512], axis=AX.X), inc=ve)
                R.wait("vector", ve)
                if half == 0:
                    R.emit("vector", lambda v, h=h: v.tensor_copy(
                        m2q16[0:1, h:h + 1], t11[0:1, 0:1]), inc=ve)
                else:
                    R.emit("vector", lambda v, h=h: v.tensor_tensor(
                        m2q16[0:1, h:h + 1], m2q16[0:1, h:h + 1], t11[0:1, 0:1],
                        ALU.max), inc=ve)
                R.wait("vector", ve)
        a2_done_ve = ve.n

        if dbg_tensor("qT", [128, NH * T], BF16) is not None:
            R.wait("sync", ve, at=a2_done_ve)
            R.emit("sync", lambda sy: sy.dma_start(out=dbg_ap["qT"], in_=qT_sb[:]), inc=cst)

        R.replay()
        sc_a.close()
        sc_ropes_q.close()

        sc_kv = ExitStack()
        kT_sb = sc_kv.enter_context(nc.sbuf_tensor("kT_sb", [128, NKV * S], BF16))
        v_sb = sc_kv.enter_context(nc.sbuf_tensor("v_sb", [128, (S // 128) * NKV * HD],
                                                  BF16))
        sc_a1 = ExitStack()
        xnT_g = sc_a1.enter_context(nc.sbuf_tensor("xnT_g", [128, 2 * DC * 512], BF16))
        wk_sb = sc_a1.enter_context(nc.sbuf_tensor("wk_sb", [128, DC * 512], BF16))
        wv_sb = sc_a1.enter_context(nc.sbuf_tensor("wv_sb", [128, DC * 512], BF16))
        x_in = sc_a1.enter_context(nc.sbuf_tensor("x_in1", [128, 3 * D], BF16))
        xn_bf = sc_a1.enter_context(nc.sbuf_tensor("xn_bf1", [128, 3 * D], BF16))
        rope_scr = sc_a1.enter_context(nc.sbuf_tensor("rope_scr1", [128, 1024], BF16))
        ps_mm0 = sc_a1.enter_context(nc.psum_tensor("ps1_mm0", [128, 512], F32))
        ps_mm1 = sc_a1.enter_context(nc.psum_tensor("ps1_mm1", [128, 512], F32))
        ps_tr0 = sc_a1.enter_context(nc.psum_tensor("ps1_tr0", [128, 1024], BF16))
        ps_tr1 = sc_a1.enter_context(nc.psum_tensor("ps1_tr1", [128, 1024], BF16))
        ps_m2 = sc_a1.enter_context(nc.psum_tensor("ps1_m2", [128, 512], F32))
        ps_mm = [ps_mm0, ps_mm1]
        ps_tr = [ps_tr0, ps_tr1]

        for b in range(2):
            src = wk_t[b * 1024:(b + 1) * 1024, :].rearrange("(c p) q -> p c q", p=128)
            R.emit("gpsimd", lambda g, s=src, b=b: g.dma_start(
                out=wk_sb[:, b * 8 * 512:(b + 1) * 8 * 512].rearrange("p (c q) -> p c q", c=8),
                in_=s), inc=cw)
        for b in range(2):
            src = wv_t[b * 1024:(b + 1) * 1024, :].rearrange("(c p) q -> p c q", p=128)
            R.emit("gpsimd", lambda g, s=src, b=b: g.dma_start(
                out=wv_sb[:, b * 8 * 512:(b + 1) * 8 * 512].rearrange("p (c q) -> p c q", c=8),
                in_=s), inc=cw)
        wkv_loaded = cw.n

        grp_pe_done = []
        xn_ready = []
        kv_tail = [None]
        a1_ln_ve = []
        a1_ln_se = []
        tr_pe_tile1 = []
        evac_se1 = []

        def emit_kv(g):
            gb = g % 2
            goff = gb * DC * 512
            rope_ve = []
            vevac_ve = []
            for g4 in range(NKV):
                b = g4 % 2
                if g == 0 and g4 == 0:
                    R.wait("tensor", cw, at=wkv_loaded)
                R.wait("tensor", se, at=xn_ready[g])
                if g4 >= 2:
                    R.wait("tensor", ve, at=rope_ve[g4 - 2])
                elif kv_tail[0] is not None:
                    R.wait("tensor", ve, at=kv_tail[0][g4])
                for c in range(DC):
                    R.emit("tensor", lambda t, g4=g4, c=c, b=b, goff=goff: t.matmul(
                        ps_mm[b][:],
                        wk_sb[:, c * 512 + g4 * 128:c * 512 + (g4 + 1) * 128],
                        xnT_g[:, goff + c * 512:goff + (c + 1) * 512],
                        start=(c == 0), stop=(c == DC - 1)),
                        inc=pe if c == DC - 1 else None)
                R.wait("vector", pe)
                emit_rope(kT_sb[:, g4 * S + g * 512:g4 * S + (g + 1) * 512],
                          ps_mm[b][:], tabC_k, tabS2_k, g * 512, 512, rope_scr, inc=ve)
                rope_ve.append(ve.n)
            for tl in range(4):
                kc = g * 4 + tl
                b = tl % 2
                if tl < 2:
                    R.wait("tensor", ve, at=rope_ve[2 + tl])
                else:
                    R.wait("tensor", ve, at=vevac_ve[tl - 2])
                for c in range(DC):
                    R.emit("tensor", lambda t, tl=tl, c=c, b=b, goff=goff: t.matmul(
                        ps_mm[b][:],
                        xnT_g[:, goff + c * 512 + tl * 128:goff + c * 512 + (tl + 1) * 128],
                        wv_sb[:, c * 512:(c + 1) * 512],
                        start=(c == 0), stop=(c == DC - 1)),
                        inc=pe if c == DC - 1 else None)
                R.wait("vector", pe)
                R.emit("vector", lambda v, kc=kc, b=b: v.tensor_copy(
                    v_sb[:, kc * 512:(kc + 1) * 512], ps_mm[b][:]), inc=ve)
                vevac_ve.append(ve.n)
            kv_tail[0] = vevac_ve[2:4]
            grp_pe_done.append(pe.n)

        def a1_emit_ln(tc):
            if True:
                buf = tc % 3
                if tc >= 3:
                    R.wait("gpsimd", ve, at=a1_ln_ve[tc - 3])
                    R.wait("gpsimd", se, at=a1_ln_se[tc - 3])
                R.emit("gpsimd", lambda gg, tc=tc, buf=buf: gg.dma_start(
                    out=x_in[:, buf * D:(buf + 1) * D],
                    in_=x_slab[tc * 128:(tc + 1) * 128, :]), inc=cin[buf])
                R.wait("vector", cin[buf])
                srcx = x_in[:, buf * D:(buf + 1) * D]
                if tc >= 3:
                    R.wait("vector", pe, at=tr_pe_tile1[tc - 3])
                    R.wait("scalar", pe, at=tr_pe_tile1[tc - 3])
                emit_ln_stats(srcx, bix=tc % 2, sq_dst=xn_bf[:, buf * D:(buf + 1) * D])
                a1_ln_se.append(se.n)
                emit_ln_apply(xn_bf[:, buf * D:(buf + 1) * D], srcx, inc=ve, bix=tc % 2)
                a1_ln_ve.append(ve.n)

        def a1_emit_tr(tc):
            if True:
                g = tc // 4
                tl = tc % 4
                gb = g % 2
                goff = gb * DC * 512
                buf = tc % 3
                for c in range(DC):
                    gidx = tc * DC + c
                    R.wait("tensor", ve, at=a1_ln_ve[tc])
                    if gidx >= 2:
                        R.wait("tensor", se, at=evac_se1[gidx - 2])
                    if tl == 0 and c == 0 and g >= 2:
                        R.wait("tensor", pe, at=grp_pe_done[g - 2])
                    R.emit("tensor", lambda t, c=c, buf=buf: t.transpose(
                        ps_tr[c % 2][:, 0:128],
                        xn_bf[:, buf * D + c * 128:buf * D + (c + 1) * 128],
                        ident[:]), inc=pe)
                    R.wait("scalar", pe)
                    R.emit("scalar", lambda s, c=c, tl=tl, goff=goff: s.copy(
                        xnT_g[:, goff + c * 512 + tl * 128:goff + c * 512 + (tl + 1) * 128],
                        ps_tr[c % 2][:, 0:128]), inc=se)
                    evac_se1.append(se.n)
                tr_pe_tile1.append(pe.n)
                if tl == 3:
                    xn_ready.append(se.n)

        for tc in range(32):
            a1_emit_ln(tc)
            if tc >= 2:
                a1_emit_tr(tc - 2)
                if (tc - 2) % 4 == 3:
                    emit_kv((tc - 2) // 4)
        a1_emit_tr(30)
        a1_emit_tr(31)
        emit_kv(7)

        R.wait("scalar", ve)
        m2_pe = 0
        for g4 in range(NKV):
            for sp in range(8):
                if not (g4 == 0 and sp == 0):
                    R.wait("scalar", pe, at=m2_pe)
                R.emit("scalar", lambda s, g4=g4, sp=sp: s.activation(
                    xn_bf[:, 0:512],
                    kT_sb[:, g4 * S + sp * 512:g4 * S + (sp + 1) * 512],
                    AF.Square), inc=se)
                R.wait("tensor", se)
                if not (g4 == 0 and sp == 0):
                    R.wait("tensor", ve)
                R.emit("tensor", lambda t: t.matmul(
                    ps_m2[0:1, 0:512], ones_col[:], xn_bf[:, 0:512],
                    start=True, stop=True), inc=pe)
                m2_pe = pe.n
                R.wait("vector", pe)
                R.emit("vector", lambda v: v.reduce_max(t11[0:1, 0:1], ps_m2[0:1, 0:512],
                                                        axis=AX.X), inc=ve)
                R.wait("vector", ve)
                if sp == 0:
                    R.emit("vector", lambda v, g4=g4: v.tensor_copy(
                        m2k4[0:1, g4:g4 + 1], t11[0:1, 0:1]), inc=ve)
                else:
                    R.emit("vector", lambda v, g4=g4: v.tensor_tensor(
                        m2k4[0:1, g4:g4 + 1], m2k4[0:1, g4:g4 + 1], t11[0:1, 0:1],
                        ALU.max), inc=ve)
        for h in range(NH):
            R.emit("vector", lambda v, h=h: v.tensor_tensor(
                t11[0:1, h:h + 1], m2q16[0:1, h:h + 1],
                m2k4[0:1, h // 4:h // 4 + 1], ALU.mult),
                inc=ve if h == NH - 1 else None)
        R.wait("scalar", ve)
        R.emit("scalar", lambda s: s.activation(t11[0:1, 0:16], t11[0:1, 0:16], AF.Sqrt,
                                                scale=1.0 / HD), inc=se)
        R.wait("vector", se)
        R.emit("vector", lambda v: v.tensor_scalar(t11[0:1, 0:16], t11[0:1, 0:16],
                                                   -1.0, None, ALU.mult), inc=ve)
        R.wait("tensor", ve)
        R.emit("tensor", lambda t: t.matmul(ps_m2[:, 0:16], onesf_row[:],
                                            t11[0:1, 0:16], start=True, stop=True),
               inc=pe)
        R.wait("vector", pe)
        R.emit("vector", lambda v: v.tensor_copy(shift_sb[:], ps_m2[:, 0:16]), inc=ve)
        a1_done_ve = ve.n

        if dbg_tensor("kT", [128, NKV * S], BF16) is not None:
            dbg_tensor("vsb", [128, (S // 128) * NKV * HD], BF16)
            dbg_tensor("shift", [128, NH])
            R.wait("sync", ve, at=a1_done_ve)
            R.emit("sync", lambda sy: sy.dma_start(out=dbg_ap["kT"], in_=kT_sb[:]), inc=cst)
            R.emit("sync", lambda sy: sy.dma_start(out=dbg_ap["vsb"], in_=v_sb[:]), inc=cst)
            R.emit("sync", lambda sy: sy.dma_start(out=dbg_ap["shift"], in_=shift_sb[:]),
                   inc=cst)

        R.replay()
        sc_a1.close()

        sc_b = ExitStack()
        sc_wo = ExitStack()
        wo_sb = sc_wo.enter_context(nc.sbuf_tensor("wo_sb", [128, DC * D], BF16,
                                                   side="right"))
        for b in range(4):
            srcw = wo_t[b * 512:(b + 1) * 512, :].rearrange("(c p) q -> p c q", p=128)
            R.emit("gpsimd", lambda g, s=srcw, b=b: g.dma_start(
                out=wo_sb[:, b * 4 * D:(b + 1) * 4 * D].rearrange("p (c q) -> p c q", c=4),
                in_=s), inc=cw)
        wo_loaded = cw.n
        pT = sc_b.enter_context(nc.sbuf_tensor("pT", [128, 3 * 1024], BF16))
        bc_sb = sc_b.enter_context(nc.sbuf_tensor("bc_sb", [128, 512], F32))
        dinv = sc_b.enter_context(nc.sbuf_tensor("dinv", [1, 512], BF16))
        oT_st = sc_b.enter_context(nc.sbuf_tensor("oT_st", [128, 2 * 512], BF16))
        ps_s0 = sc_b.enter_context(nc.psum_tensor("ps_s0", [128, 1024], F32))
        ps_s1 = sc_b.enter_context(nc.psum_tensor("ps_s1", [128, 1024], F32))
        ps_s2 = sc_b.enter_context(nc.psum_tensor("ps_s2", [128, 1024], F32))
        ps_pv = sc_b.enter_context(nc.psum_tensor("ps_pv", [128, 512], F32))
        ps_den = sc_b.enter_context(nc.psum_tensor("ps_den", [128, 512], F32))
        ps_bc = ps_den
        ps_s = [ps_s0, ps_s1, ps_s2]

        R.wait("tensor", ve, at=a1_done_ve)
        R.wait("scalar", ve, at=a1_done_ve)
        NP = 16
        ospill_cst = []
        prev_exp_se = [None]
        prev_cons_pe = [None]
        prev_evac_ve = [None]
        pending_tail = [None]
        for h in range(NH):
            g4 = h // NKV
            for qt in range(2):
                it = h * 2 + qt
                q0 = qt * 512
                sbuf_b = it % 2
                exp_se = []
                cons_pe = []

                def b_emit_s(p):
                    b = p % 3
                    if p >= 3:
                        R.wait("tensor", se, at=exp_se[p - 3])
                    elif prev_exp_se[0] is not None:
                        R.wait("tensor", se,
                               at=prev_exp_se[0][NP - 1 - ((NP - 1 - p) % 3)])
                    for half in range(2):
                        kc = 2 * p + half
                        R.emit("tensor", lambda t, g4=g4, kc=kc, b=b, half=half, h=h,
                               q0=q0: t.matmul(
                                   ps_s[b][:, half * 512:(half + 1) * 512],
                                   kT_sb[:, g4 * S + kc * 128:g4 * S + (kc + 1) * 128],
                                   qT_sb[:, h * T + q0:h * T + q0 + 512],
                                   start=True, stop=True),
                               inc=pe if half == 1 else None)
                    R.wait("scalar", pe)
                    if p >= 3:
                        R.wait("scalar", pe, at=cons_pe[p - 3])
                    elif prev_cons_pe[0] is not None:
                        R.wait("scalar", pe,
                               at=prev_cons_pe[0][NP - 1 - ((NP - 1 - p) % 3)])
                    R.emit("scalar", lambda s, h=h, b=b: s.activation(
                        pT[:, b * 1024:(b + 1) * 1024], ps_s[b][:], AF.Exp,
                        bias=shift_sb[:, h:h + 1], scale=RSQ), inc=se)
                    exp_se.append(se.n)

                def b_emit_dpv(p):
                    b = p % 3
                    R.wait("tensor", se, at=exp_se[p])
                    if p == 0 and prev_evac_ve[0] is not None:
                        R.wait("tensor", ve, at=prev_evac_ve[0])
                    for half in range(2):
                        kc = 2 * p + half
                        R.emit("tensor", lambda t, b=b, half=half, kc=kc: t.matmul(
                            ps_den[0:1, 0:512], ones_col[:],
                            pT[:, b * 1024 + half * 512:b * 1024 + (half + 1) * 512],
                            start=(kc == 0), stop=(kc == 31)))
                        R.emit("tensor", lambda t, b=b, half=half, kc=kc, g4=g4: t.matmul(
                            ps_pv[:],
                            v_sb[:, kc * 512 + g4 * 128:kc * 512 + (g4 + 1) * 128],
                            pT[:, b * 1024 + half * 512:b * 1024 + (half + 1) * 512],
                            start=(kc == 0), stop=(kc == 31)),
                            inc=pe if half == 1 else None)
                    cons_pe.append(pe.n)

                def b_emit_tail(h=h, q0=q0, sbuf_b=sbuf_b, it=it):
                    R.wait("vector", pe)
                    def _recip(v):
                        with nc.allow_low_precision(reason="softmax denominator bcast in bf16; tolerance sized for bf16 matmul path"):
                            return v.reciprocal(dinv[0:1, :], ps_den[0:1, 0:512])
                    R.emit("vector", _recip, inc=ve)
                    R.wait("tensor", ve)
                    R.emit("tensor", lambda t: t.matmul(ps_bc[:], ones_bfr[0:1, :],
                                                        dinv[0:1, :],
                                                        start=True, stop=True), inc=pe)
                    R.wait("vector", pe)
                    if it >= 2:
                        R.wait("vector", cst, at=ospill_cst[it - 2])
                    R.emit("vector", lambda v: v.tensor_copy(bc_sb[:], ps_bc[:]))
                    R.emit("vector", lambda v: v.tensor_tensor(
                        oT_st[:, sbuf_b * 512:(sbuf_b + 1) * 512], ps_pv[:], bc_sb[:],
                        ALU.mult), inc=ve)
                    prev_evac_ve[0] = ve.n
                    R.wait("scalar", ve)
                    R.emit("scalar", lambda s: s.dma_start(
                        out=oT_scr[h, :, q0:q0 + 512],
                        in_=oT_st[:, sbuf_b * 512:(sbuf_b + 1) * 512]), inc=cst)
                    ospill_cst.append(cst.n)

                for p in range(NP):
                    b_emit_s(p)
                    if p == 1 and pending_tail[0] is not None:
                        pending_tail[0]()
                        pending_tail[0] = None
                    if p >= 2:
                        b_emit_dpv(p - 2)
                b_emit_dpv(NP - 2)
                b_emit_dpv(NP - 1)
                prev_exp_se[0] = exp_se
                prev_cons_pe[0] = cons_pe
                pending_tail[0] = b_emit_tail
        pending_tail[0]()
        pending_tail[0] = None
        b_done_cst = cst.n
        b_done_ve = ve.n

        if dbg_tensor("oT", [NH, 128, T], BF16) is not None:
            R.wait("sync", cst, at=b_done_cst)
            R.emit("sync", lambda sy: sy.dma_start(out=dbg_ap["oT"], in_=oT_scr), inc=cst)

        R.replay()
        sc_b.close()
        sc_kv.close()
        sc_qT.close()
        sc_ropes.close()

        sc_cde = ExitStack()
        xn2T = sc_cde.enter_context(nc.sbuf_tensor("xn2T", [128, DC * T], BF16))
        sc_c = ExitStack()
        oT_in = sc_c.enter_context(nc.sbuf_tensor("oT_in", [128, 2 * NH * 512], BF16))
        xo_in = sc_c.enter_context(nc.sbuf_tensor("xo_in", [128, D], F32))
        x1_buf = sc_c.enter_context(nc.sbuf_tensor("x1_buf", [128, 2 * D], F32))
        xn2_bf = sc_c.enter_context(nc.sbuf_tensor("xn2_bf", [128, NTC * D], BF16))
        ps_oA = sc_c.enter_context(nc.psum_tensor("ps_oA", [128, 2048], F32))
        ps_oB = sc_c.enter_context(nc.psum_tensor("ps_oB", [128, 2048], F32))
        ps_o = [ps_oA, ps_oB]
        tr_v = [ps_oA[:].bitcast(BF16)[:, 0:128], ps_oA[:].bitcast(BF16)[:, 1024:1152]]

        for qt in range(2):
            srcq = oT_scr[:, :, qt * 512:(qt + 1) * 512].rearrange("h p t -> p h t")
            if qt == 0:
                R.wait("sync", cst, at=b_done_cst)
            R.emit("sync", lambda sy, s=srcq, qt=qt: sy.dma_start(
                out=oT_in[:, qt * NH * 512:(qt + 1) * NH * 512].rearrange(
                    "p (h t) -> p h t", h=NH), in_=s), inc=cot[qt])

        c_add_ve = []
        c_ln2ap_ve = []
        c_ln3_ve = []
        c_spill_cst = []
        for tc in range(NTC):
            b = tc % 2
            buf = tc % 2
            qt = tc // 4
            if tc >= 1:
                R.wait("sync", ve, at=c_add_ve[tc - 1])
            R.emit("sync", lambda sy, tc=tc: sy.dma_start(
                out=xo_in[:], in_=x_own[tc * 128:(tc + 1) * 128, :]), inc=cin[0])
            if tc == 0:
                R.wait("tensor", cw, at=wo_loaded)
            R.wait("tensor", cot[qt])
            if tc >= 2:
                R.wait("tensor", ve, at=c_ln2ap_ve[tc - 2])
            for h in range(NH):
                for jb in range(4):
                    R.emit("tensor", lambda t, h=h, jb=jb, b=b, qt=qt, tc=tc: t.matmul(
                        ps_o[b][:, jb * 512:(jb + 1) * 512],
                        oT_in[:, qt * NH * 512 + h * 512 + (tc % 4) * 128:
                              qt * NH * 512 + h * 512 + (tc % 4) * 128 + 128],
                        wo_sb[:, h * D + jb * 512:h * D + (jb + 1) * 512],
                        start=(h == 0), stop=(h == NH - 1)),
                        inc=pe if (h == NH - 1 and jb == 3) else None)
            R.wait("vector", pe)
            if tc >= 2:
                R.wait("vector", cst, at=c_spill_cst[tc - 2])
                R.wait("scalar", cst, at=c_spill_cst[tc - 2])
                R.wait("scalar", ve, at=c_ln3_ve[tc - 2])
            emit_ln_stats(ps_o[b][:], bix=buf, sq_dst=x1_buf[:, buf * D:(buf + 1) * D])
            emit_ln_apply(x1_buf[:, buf * D:(buf + 1) * D], ps_o[b][:], inc=ve, bix=buf)
            c_ln2ap_ve.append(ve.n)
            R.wait("vector", cin[0])
            R.emit("vector", lambda v, buf=buf: v.tensor_add(
                x1_buf[:, buf * D:(buf + 1) * D], x1_buf[:, buf * D:(buf + 1) * D],
                xo_in[:]), inc=ve)
            c_add_ve.append(ve.n)
            R.wait("scalar", ve)
            R.emit("scalar", lambda s, tc=tc, buf=buf: s.dma_start(
                out=x1_scr[tc * 128:(tc + 1) * 128, :],
                in_=x1_buf[:, buf * D:(buf + 1) * D]), inc=cst)
            c_spill_cst.append(cst.n)
            src2 = x1_buf[:, buf * D:(buf + 1) * D]
            emit_ln_stats(src2, bix=buf, sq_dst=xn2_bf[:, tc * D:(tc + 1) * D])
            emit_ln_apply(xn2_bf[:, tc * D:(tc + 1) * D], src2, inc=ve, bix=buf)
            c_ln3_ve.append(ve.n)
        c_tr_se = []
        R.wait("tensor", ve)
        for tc in range(NTC):
            for c in range(DC):
                gidx = tc * DC + c
                if gidx >= 2:
                    R.wait("tensor", se, at=c_tr_se[gidx - 2])
                R.emit("tensor", lambda t, tc=tc, c=c: t.transpose(
                    tr_v[(tc * DC + c) % 2],
                    xn2_bf[:, tc * D + c * 128:tc * D + (c + 1) * 128],
                    ident[:]), inc=pe)
                R.wait("scalar", pe)
                R.emit("scalar", lambda s, c=c, tc=tc, gidx=gidx: s.copy(
                    xn2T[:, c * T + tc * 128:c * T + (tc + 1) * 128], tr_v[gidx % 2]),
                    inc=se)
                c_tr_se.append(se.n)
        c_done_se = se.n
        c_x1_cst = cst.n

        if dbg_tensor("xn2T", [128, DC * T], BF16) is not None:
            dbg_tensor("x1", [T, D])
            R.wait("sync", se, at=c_done_se)
            R.wait("sync", cst, at=c_x1_cst)
            R.emit("sync", lambda sy: sy.dma_start(out=dbg_ap["xn2T"], in_=xn2T[:]),
                   inc=cst)
            R.emit("sync", lambda sy: sy.dma_start(out=dbg_ap["x1"], in_=x1_scr), inc=cst)

        R.replay()
        sc_c.close()
        sc_wo.close()

        sc_h = ExitStack()
        h_sb = sc_h.enter_context(nc.sbuf_tensor("h_sb", [128, ICN * T], BF16))
        sc_d = ExitStack()
        gs_sb = sc_d.enter_context(nc.sbuf_tensor("gs_sb", [128, ICN], F32))
        r2_sb = sc_d.enter_context(nc.sbuf_tensor("r2_sb", [2, T], BF16))
        rh_t = sc_d.enter_context(nc.sbuf_tensor("rh_t", [1, T], BF16))
        rl_t = sc_d.enter_context(nc.sbuf_tensor("rl_t", [1, T], BF16))
        rb_t = sc_d.enter_context(nc.sbuf_tensor("rb_t", [1, T], F32))
        us_sb = sc_d.enter_context(nc.sbuf_tensor("us_sb", [128, ICN], F32))
        wg_buf = sc_d.enter_context(nc.sbuf_tensor("wg_buf", [128, 2 * D], BF16))
        wu_buf = sc_d.enter_context(nc.sbuf_tensor("wu_buf", [128, 2 * D], BF16))
        gs_bf = sc_d.enter_context(nc.sbuf_tensor("gs_bf", [128, 2 * 512], BF16))
        us_bf = sc_d.enter_context(nc.sbuf_tensor("us_bf", [128, 2 * 512], BF16))
        ps_g0 = sc_d.enter_context(nc.psum_tensor("ps_g0", [128, 512], F32))
        ps_g1 = sc_d.enter_context(nc.psum_tensor("ps_g1", [128, 512], F32))
        ps_u0 = sc_d.enter_context(nc.psum_tensor("ps_u0", [128, 512], F32))
        ps_u1 = sc_d.enter_context(nc.psum_tensor("ps_u1", [128, 512], F32))
        ps_r = sc_d.enter_context(nc.psum_tensor("ps_r", [128, 512], F32))
        ps_hs = sc_d.enter_context(nc.psum_tensor("ps_hs", [128, 512], F32))
        ps_gq = [ps_g0, ps_g1]
        ps_uq = [ps_u0, ps_u1]

        R.emit("sync", lambda sy: sy.dma_start(out=gs_sb[:], in_=gs_t), inc=cw)
        R.emit("sync", lambda sy: sy.dma_start(out=us_sb[:], in_=us_t), inc=cw)
        dscales_loaded = cw.n

        R.wait("tensor", se, at=c_done_se)
        for qt in range(2):
            if qt == 1:
                R.wait("tensor", ve)
            for c in range(DC):
                R.emit("tensor", lambda t, qt=qt, c=c: t.matmul(
                    ps_r[0:1, 0:512], ones_col[:],
                    xn2T[:, c * T + qt * 512:c * T + (qt + 1) * 512],
                    start=(c == 0), stop=(c == DC - 1)),
                    inc=pe if c == DC - 1 else None)
            R.wait("vector", pe)
            R.emit("vector", lambda v, qt=qt: v.tensor_copy(
                rr_sb[0:1, qt * 512:(qt + 1) * 512], ps_r[0:1, 0:512]), inc=ve)
        R.wait("vector", ve)
        R.emit("vector", lambda v: v.tensor_copy(rh_t[0:1, :], rr_sb[0:1, :]), inc=ve)
        R.wait("vector", ve)
        R.emit("vector", lambda v: v.tensor_copy(rb_t[0:1, :], rh_t[0:1, :]), inc=ve)
        R.wait("vector", ve)
        R.emit("vector", lambda v: v.tensor_sub(rl_t[0:1, :], rr_sb[0:1, :],
                                                rb_t[0:1, :]), inc=ve)
        R.wait("sync", ve)
        R.emit("sync", lambda sy: sy.dma_start(out=r2_sb[0:1, :], in_=rh_t[0:1, :]),
               inc=cw)
        R.emit("sync", lambda sy: sy.dma_start(out=r2_sb[1:2, :], in_=rl_t[0:1, :]),
               inc=cw)
        r2_cw = cw.n
        R.wait("tensor", cw, at=r2_cw)
        r2_ve = ve.n

        silu_se = {}
        umul_ve = {}
        dma_pe_gate = []
        for ic in range(ICN):
            bb = ic % 2
            if ic >= 2:
                R.wait("gpsimd", pe, at=dma_pe_gate[ic - 2])
            R.emit("gpsimd", lambda g, ic=ic, bb=bb: g.dma_start(
                out=wg_buf[:, bb * D:(bb + 1) * D], in_=wg_t4[ic]), inc=cwg[bb])
            R.emit("gpsimd", lambda g, ic=ic, bb=bb: g.dma_start(
                out=wu_buf[:, bb * D:(bb + 1) * D], in_=wu_t4[ic]), inc=cwu[bb])
            R.wait("tensor", cwg[bb])
            R.wait("tensor", cwu[bb])
            if ic == 0:
                R.wait("tensor", ve, at=r2_ve)
            for qt in range(2):
                for mat in range(2):
                    wbuf = wg_buf if mat == 0 else wu_buf
                    psx = ps_gq[qt] if mat == 0 else ps_uq[qt]
                    if ic >= 1:
                        if mat == 0:
                            R.wait("tensor", se, at=silu_se[(ic - 1, qt)])
                        else:
                            R.wait("tensor", ve, at=umul_ve[(ic - 1, qt)])
                    for c in range(DC):
                        R.emit("tensor", lambda t, wbuf=wbuf, psx=psx, bb=bb, c=c, qt=qt:
                               t.matmul(psx[:],
                                        wbuf[:, bb * D + c * 128:bb * D + (c + 1) * 128],
                                        xn2T[:, c * T + qt * 512:c * T + (qt + 1) * 512],
                                        start=(c == 0), stop=False))
                    R.emit("tensor", lambda t, psx=psx, qt=qt: t.matmul(
                        psx[:], c128_2[0:2, :], r2_sb[0:2, qt * 512:(qt + 1) * 512],
                        start=False, stop=True), inc=pe)
                    if mat == 0:
                        R.wait("scalar", pe)
                        if ic == 0 and qt == 0:
                            R.wait("scalar", cw, at=dscales_loaded)
                        if ic >= 2:
                            R.wait("scalar", ve, at=umul_ve[(ic - 2, qt)])
                        R.emit("scalar", lambda s, ic=ic, qt=qt, bb=bb: s.activation(
                            gs_bf[:, bb * 512:(bb + 1) * 512], ps_gq[qt][:], AF.Silu,
                            scale=gs_sb[:, ic:ic + 1]), inc=se)
                        silu_se[(ic, qt)] = se.n
                    else:
                        R.wait("vector", pe)
                        if ic >= 2:
                            R.wait("vector", ve, at=umul_ve[(ic - 2, qt)])
                        R.emit("vector", lambda v, ic=ic, qt=qt, bb=bb: v.tensor_scalar(
                            us_bf[:, bb * 512:(bb + 1) * 512], ps_uq[qt][:],
                            us_sb[:, ic:ic + 1], None, ALU.mult))
                        R.wait("vector", se, at=silu_se[(ic, qt)])
                        R.emit("vector", lambda v, ic=ic, qt=qt, bb=bb: v.tensor_tensor(
                            h_sb[:, ic * T + qt * 512:ic * T + (qt + 1) * 512],
                            gs_bf[:, bb * 512:(bb + 1) * 512],
                            us_bf[:, bb * 512:(bb + 1) * 512], ALU.mult), inc=ve)
                        umul_ve[(ic, qt)] = ve.n
            dma_pe_gate.append(pe.n)
            icd = ic - 1 if ic >= 1 else None
            if icd is not None:
                R.wait("tensor", ve, at=umul_ve[(icd, 1)])
                for tcx in range(NTC):
                    R.emit("tensor", lambda t, icd=icd, tcx=tcx: t.matmul(
                        ps_hs[:, tcx:tcx + 1],
                        h_sb[:, icd * T + tcx * 128:icd * T + (tcx + 1) * 128],
                        ones_col[:], start=(icd == 0 and tcx == 0), stop=False),
                        inc=pe if tcx == NTC - 1 else None)
        R.wait("tensor", ve, at=umul_ve[(ICN - 1, 1)])
        for tcx in range(NTC):
            R.emit("tensor", lambda t, tcx=tcx: t.matmul(
                ps_hs[:, tcx:tcx + 1],
                h_sb[:, (ICN - 1) * T + tcx * 128:(ICN - 1) * T + tcx * 128 + 128],
                ones_col[:], start=False, stop=True),
                inc=pe if tcx == NTC - 1 else None)
        R.wait("vector", pe)
        R.emit("vector", lambda v: v.tensor_scalar(hs128[:, 0:NTC], ps_hs[:, 0:NTC],
                                                   -128.0, None, ALU.mult), inc=ve)
        d_done_ve = ve.n

        if dbg_tensor("hsb", [128, 4 * T], BF16) is not None:
            R.wait("sync", ve, at=d_done_ve)
            R.emit("sync", lambda sy: sy.dma_start(out=dbg_ap["hsb"],
                                                   in_=h_sb[:, 0:4 * T]), inc=cst)
            dbg_tensor("hs128", [128, NTC])
            R.emit("sync", lambda sy: sy.dma_start(out=dbg_ap["hs128"], in_=hs128[:]),
                   inc=cst)

        R.replay()
        sc_d.close()

        sc_e = ExitStack()
        ds_sb = sc_e.enter_context(nc.sbuf_tensor("ds_sb", [128, D], F32))
        wd_buf = sc_e.enter_context(nc.sbuf_tensor("wd_buf", [128, 2 * 4 * 512], BF16))
        yo_buf = sc_e.enter_context(nc.sbuf_tensor("yo_buf", [128, 4 * 512], F32))
        ps_y = [sc_e.enter_context(nc.psum_tensor(f"ps_y{i}", [128, 512], F32))
                for i in range(NTC)]

        R.emit("sync", lambda sy: sy.dma_start(out=ds_sb[:], in_=ds_b), inc=cw)
        ds_loaded = cw.n
        R.wait("tensor", ve, at=d_done_ve)

        e_evac_ve = {}
        e_spill_cst = {}
        e_mm_pe = {}
        for jq in range(4):
            for icc in range(16):
                bb = icc % 2
                if jq * 16 + icc >= 2:
                    pj, pi = divmod(jq * 16 + icc - 2, 16)
                    R.wait("gpsimd", pe, at=e_dma_gate[(pj, pi)])
                src = wd_t[icc * 4:(icc + 1) * 4, :, jq * 512:(jq + 1) * 512].rearrange(
                    "a p f -> p a f")
                R.emit("gpsimd", lambda g, s=src, bb=bb: g.dma_start(
                    out=wd_buf[:, bb * 4 * 512:(bb + 1) * 4 * 512].rearrange(
                        "p (a f) -> p a f", a=4), in_=s), inc=cwd[bb])
                R.wait("tensor", cwd[bb])
                for l in range(4):
                    ic = icc * 4 + l
                    for tc in range(NTC):
                        if ic == 0 and jq > 0:
                            R.wait("tensor", ve, at=e_evac_ve[(jq - 1, tc)])
                        R.emit("tensor", lambda t, ic=ic, tc=tc, bb=bb, l=l: t.matmul(
                            ps_y[tc][:],
                            h_sb[:, ic * T + tc * 128:ic * T + (tc + 1) * 128],
                            wd_buf[:, (bb * 4 + l) * 512:(bb * 4 + l + 1) * 512],
                            start=(ic == 0), stop=(ic == ICN - 1)),
                            inc=pe if (ic == ICN - 1 or (tc == NTC - 1 and l == 3))
                            else None)
                        if ic == ICN - 1:
                            e_mm_pe[(jq, tc)] = pe.n
                if icc == 0 and jq == 0:
                    e_dma_gate = {}
                e_dma_gate[(jq, icc)] = pe.n
            for tc in range(NTC):
                b4 = tc % 4
                R.wait("vector", pe, at=e_mm_pe[(jq, tc)])
                if jq == 0 and tc == 0:
                    R.wait("vector", cw, at=ds_loaded)
                if (jq, tc) != (0, 0):
                    prev = (jq, tc - 4) if tc >= 4 else (jq - 1, tc + 4)
                    if prev in e_spill_cst:
                        R.wait("vector", cst, at=e_spill_cst[prev])
                R.emit("vector", lambda v, tc=tc, jq=jq, b4=b4: v.scalar_tensor_tensor(
                    yo_buf[:, b4 * 512:(b4 + 1) * 512], ps_y[tc][:],
                    hs128[:, tc:tc + 1], ds_sb[:, jq * 512:(jq + 1) * 512],
                    ALU.add, ALU.mult), inc=ve)
                e_evac_ve[(jq, tc)] = ve.n
                R.wait("scalar", ve)
                R.emit("scalar", lambda s, tc=tc, jq=jq, b4=b4: s.dma_start(
                    out=y_scr[tc * 128:(tc + 1) * 128, jq * 512:(jq + 1) * 512],
                    in_=yo_buf[:, b4 * 512:(b4 + 1) * 512]), inc=cst)
                e_spill_cst[(jq, tc)] = cst.n

        R.replay()
        sc_e.close()
        sc_h.close()
        sc_cde.close()

        sc_f = ExitStack()
        fy = sc_f.enter_context(nc.sbuf_tensor("fy", [128, 2 * D], F32))
        fx = sc_f.enter_context(nc.sbuf_tensor("fx", [128, 2 * D], F32))
        fo = sc_f.enter_context(nc.sbuf_tensor("fo", [128, 2 * D], F32))

        f_ve = []
        for tc in range(NTC):
            buf = tc % 2
            R.wait("sync", cst, at=e_spill_cst[(3, tc)])
            R.wait("sync", cst, at=c_x1_cst)
            if tc >= 2:
                R.wait("sync", ve, at=f_ve[tc - 2])
            R.emit("sync", lambda sy, tc=tc, buf=buf: sy.dma_start(
                out=fy[:, buf * D:(buf + 1) * D],
                in_=y_scr[tc * 128:(tc + 1) * 128, :]), inc=cfy[buf])
            R.emit("sync", lambda sy, tc=tc, buf=buf: sy.dma_start(
                out=fx[:, buf * D:(buf + 1) * D],
                in_=x1_scr[tc * 128:(tc + 1) * 128, :]), inc=cfx[buf])
            R.wait("vector", cfy[buf])
            R.wait("scalar", cfy[buf])
            src = fy[:, buf * D:(buf + 1) * D]
            if tc >= 2:
                R.wait("vector", cst, at=f_out_cst[tc - 2])
                R.wait("scalar", cst, at=f_out_cst[tc - 2])
            emit_ln_stats(src, bix=buf, sq_dst=fo[:, buf * D:(buf + 1) * D])
            emit_ln_apply(fo[:, buf * D:(buf + 1) * D], src, bix=buf)
            R.wait("vector", cfx[buf])
            R.emit("vector", lambda v, buf=buf: v.tensor_add(
                fo[:, buf * D:(buf + 1) * D], fo[:, buf * D:(buf + 1) * D],
                fx[:, buf * D:(buf + 1) * D]), inc=ve)
            f_ve.append(ve.n)
            R.wait("scalar", ve)
            R.emit("scalar", lambda s, tc=tc, buf=buf: s.dma_start(
                out=out[tc * 128:(tc + 1) * 128, :],
                in_=fo[:, buf * D:(buf + 1) * D]), inc=cst)
            if tc == 0:
                f_out_cst = []
            f_out_cst.append(cst.n)

        R.wait("sync", cst)
        R.replay()
        sc_f.close()

    return nc, dbg_ap


def make_in_maps(inputs):
    x = np.asarray(inputs["L0_attn_hidden_states"], dtype=np.float32)
    wq = np.asarray(inputs["L0_attn_Wq"], dtype=np.float32)
    wk = np.asarray(inputs["L0_attn_Wk"], dtype=np.float32)
    wv = np.asarray(inputs["L0_attn_Wv"], dtype=np.float32)
    wo = np.asarray(inputs["L0_attn_Wo"], dtype=np.float32)
    gw = np.asarray(inputs["L0_ffn_gate_w"], dtype=np.int32)
    uw = np.asarray(inputs["L0_ffn_up_w"], dtype=np.int32)
    dw = np.asarray(inputs["L0_ffn_down_w"], dtype=np.int32)
    gsc = np.asarray(inputs["L0_ffn_gate_scale"], dtype=np.float32)
    usc = np.asarray(inputs["L0_ffn_up_scale"], dtype=np.float32)
    dsc = np.asarray(inputs["L0_ffn_down_scale"], dtype=np.float32)

    wq_t = np.ascontiguousarray(wq.T)
    wk_t = np.ascontiguousarray(wk.T)
    wv_t = np.ascontiguousarray(wv.T)
    wo_t = np.ascontiguousarray(wo.T)
    wg_t4 = np.ascontiguousarray(gw.reshape(ICN, 128, DC, 128).transpose(0, 3, 2, 1))
    wu_t4 = np.ascontiguousarray(uw.reshape(ICN, 128, DC, 128).transpose(0, 3, 2, 1))
    wd_t = np.ascontiguousarray(dw.T.reshape(ICN, 128, D))
    gs_t = np.ascontiguousarray(gsc.reshape(ICN, 128).T)
    us_t = np.ascontiguousarray(usc.reshape(ICN, 128).T)
    ds_bh = np.ascontiguousarray(np.broadcast_to(dsc[None, :], (128, D)))

    in_maps = []
    for core in range(8):
        bi = core // 4
        r0 = (core % 4) * T
        posq = np.ascontiguousarray(
            np.broadcast_to(np.arange(r0, r0 + T, dtype=np.float32)[None, :], (64, T)))
        in_maps.append({
            "x_slab": np.ascontiguousarray(x[bi]),
            "x_own": np.ascontiguousarray(x[bi, r0:r0 + T]),
            "pos_q": posq,
            "wq_t": wq_t, "wk_t": wk_t, "wv_t": wv_t, "wo_t": wo_t,
            "wg_t4": wg_t4, "wu_t4": wu_t4, "wd_t": wd_t,
            "gs_t": gs_t, "us_t": us_t, "ds_b": ds_bh,
        })
    return in_maps


def kernel(**inputs):
    from concourse.bass_utils import run_bass_kernel_spmd
    nc, _ = build_nc()
    in_maps = make_in_maps(inputs)
    res = run_bass_kernel_spmd(nc, in_maps, core_ids=list(range(8)))
    outp = np.empty((2, S, D), dtype=np.float32)
    for core in range(8):
        bi = core // 4
        r0 = (core % 4) * T
        outp[bi, r0:r0 + T] = res.results[core]["out"]
    return outp


# version 07 (final)
# speedup vs baseline: 1.0429x; 1.0429x over previous
import math
from contextlib import ExitStack

import numpy as np

import concourse.bass as bass
import concourse.mybir as mybir

F32 = mybir.dt.float32
BF16 = mybir.dt.bfloat16
I32 = mybir.dt.int32
AF = mybir.ActivationFunctionType
ALU = mybir.AluOpType
AX = mybir.AxisListType

D = 2048
DC = 16
S = 4096
T = 1024
NH = 16
NKV = 4
HD = 128
II = 8192
ICN = 64
NTC = 8
RSQ = 1.0 / math.sqrt(HD)
TAU = 2.0 * math.pi
EPS = 1e-5

ENGINES = ("tensor", "vector", "scalar", "gpsimd", "sync")


class Chan:
    def __init__(self, ctx, nc, name, step=1):
        self.sem = ctx.enter_context(nc.semaphore(name))
        self.n = 0
        self.step = step


class Rec:

    def __init__(self, nc):
        self.nc = nc
        self.ops = {e: [] for e in ENGINES}

    def emit(self, eng, fn, inc=None, k=None):
        if inc is not None:
            kk = inc.step if k is None else k
            inc.n += kk
            sem = inc.sem
            self.ops[eng].append(lambda e, fn=fn, sem=sem, kk=kk: fn(e).then_inc(sem, kk))
        else:
            self.ops[eng].append(fn)

    def wait(self, eng, chan, at=None):
        tgt = chan.n if at is None else at
        sem = chan.sem
        if tgt > 0:
            self.ops[eng].append(lambda e, sem=sem, tgt=tgt: e.wait_ge(sem, tgt))

    def replay(self):
        nc = self.nc
        ops = self.ops
        self.ops = {e: [] for e in ENGINES}
        with nc.Block() as blk:
            if ops["tensor"]:
                @blk.tensor
                def _(t):
                    for fn in ops["tensor"]:
                        fn(t)
            if ops["vector"]:
                @blk.vector
                def _(v):
                    for fn in ops["vector"]:
                        fn(v)
            if ops["scalar"]:
                @blk.scalar
                def _(s):
                    for fn in ops["scalar"]:
                        fn(s)
            if ops["gpsimd"]:
                @blk.gpsimd
                def _(g):
                    for fn in ops["gpsimd"]:
                        fn(g)
            if ops["sync"]:
                @blk.sync
                def _(sy):
                    for fn in ops["sync"]:
                        fn(sy)


def build_nc(debug=()):
    nc = bass.Bass("TRN2", target_bir_lowering=False)
    dbg = set(debug)

    x_slab = nc.dram_tensor("x_slab", [S, D], F32, kind="ExternalInput").ap()
    x_own = nc.dram_tensor("x_own", [T, D], F32, kind="ExternalInput").ap()
    pos_q = nc.dram_tensor("pos_q", [64, T], F32, kind="ExternalInput").ap()
    wq_t = nc.dram_tensor("wq_t", [D, D], F32, kind="ExternalInput").ap()
    wk_t = nc.dram_tensor("wk_t", [D, NKV * HD], F32, kind="ExternalInput").ap()
    wv_t = nc.dram_tensor("wv_t", [D, NKV * HD], F32, kind="ExternalInput").ap()
    wo_t = nc.dram_tensor("wo_t", [D, D], F32, kind="ExternalInput").ap()
    wg_t4 = nc.dram_tensor("wg_t4", [ICN, 128, D], I32, kind="ExternalInput").ap()
    wu_t4 = nc.dram_tensor("wu_t4", [ICN, 128, D], I32, kind="ExternalInput").ap()
    wd_t = nc.dram_tensor("wd_t", [ICN, 128, D], I32, kind="ExternalInput").ap()
    gs_t = nc.dram_tensor("gs_t", [128, ICN], F32, kind="ExternalInput").ap()
    us_t = nc.dram_tensor("us_t", [128, ICN], F32, kind="ExternalInput").ap()
    ds_b = nc.dram_tensor("ds_b", [128, D], F32, kind="ExternalInput").ap()
    out = nc.dram_tensor("out", [T, D], F32, kind="ExternalOutput").ap()
    x1_scr = nc.dram_tensor("x1_scr", [T, D], F32, kind="Internal").ap()
    y_scr = nc.dram_tensor("y_scr", [T, D], F32, kind="Internal").ap()
    oT_scr = nc.dram_tensor("oT_scr", [NH, 128, T], BF16, kind="Internal").ap()

    dbg_ap = {}

    def dbg_tensor(name, shape, dtype=F32):
        if name in dbg:
            dbg_ap[name] = nc.dram_tensor("dbg_" + name, shape, dtype,
                                          kind="ExternalOutput").ap()
        return dbg_ap.get(name)

    ctx = ExitStack()
    with ctx:
        sbp = lambda name, shape, dt: ctx.enter_context(nc.sbuf_tensor(name, shape, dt))
        ident = sbp("ident", [128, 128], BF16)
        ones_col = sbp("ones_col", [128, 1], BF16)
        onesf_row = sbp("onesf_row", [1, 128], F32)
        c128_2 = sbp("c128_2", [2, 128], BF16)
        ones_bfr = sbp("ones_bfr", [1, 128], BF16)
        eps_t = sbp("eps_t", [128, 1], F32)
        shift_sb = sbp("shift_sb", [128, NH], F32)
        m2q16 = sbp("m2q16", [1, NH], F32)
        m2k4 = sbp("m2k4", [1, NKV], F32)
        rr_sb = sbp("rr_sb", [1, T], F32)
        hs128 = sbp("hs128", [128, NTC], F32)
        mean_t2 = sbp("mean_t", [128, 2], F32)
        negm_t2 = sbp("negm_t", [128, 2], F32)
        var_t2 = sbp("var_t", [128, 2], F32)
        istd_t2 = sbp("istd_t", [128, 2], F32)
        t11 = sbp("t11", [1, 16], F32)
        dbg_ln0 = sbp("dbgln0_sb", [128, 8], F32)

        pe = Chan(ctx, nc, "pe")
        ve = Chan(ctx, nc, "ve")
        se = Chan(ctx, nc, "se")
        gp = Chan(ctx, nc, "gp")
        cin = [Chan(ctx, nc, f"cin{i}", 16) for i in range(3)]
        cw = Chan(ctx, nc, "cw", 16)
        cwg = [Chan(ctx, nc, f"cwg{i}", 16) for i in range(2)]
        cwu = [Chan(ctx, nc, f"cwu{i}", 16) for i in range(2)]
        cwd = [Chan(ctx, nc, f"cwd{i}", 16) for i in range(2)]
        cst = Chan(ctx, nc, "cst", 16)
        cfy = [Chan(ctx, nc, f"cfy{i}", 16) for i in range(2)]
        cfx = [Chan(ctx, nc, f"cfx{i}", 16) for i in range(2)]
        cot = [Chan(ctx, nc, f"cot{i}", 16) for i in range(2)]

        R = Rec(nc)

        def emit_ln_stats(src_ap, bix=0, sq_dst=None):
            mean_t = mean_t2[:, bix:bix + 1]
            negm_t = negm_t2[:, bix:bix + 1]
            var_t = var_t2[:, bix:bix + 1]
            istd_t = istd_t2[:, bix:bix + 1]
            R.emit("vector", lambda v, a=src_ap: v.reduce_sum(mean_t, a, axis=AX.X),
                   inc=ve)
            R.wait("vector", ve)
            R.emit("vector", lambda v: v.tensor_scalar(mean_t, mean_t, 1.0 / D,
                                                       None, ALU.mult), inc=ve)
            R.wait("vector", ve)
            R.emit("vector", lambda v: v.tensor_scalar(negm_t, mean_t, -1.0,
                                                       None, ALU.mult), inc=ve)
            R.wait("scalar", ve)
            R.emit("scalar", lambda s, a=src_ap, d=sq_dst: s.activation(
                d, a, AF.Square, bias=negm_t, accum_out=var_t))
            R.emit("scalar", lambda s: s.activation(var_t, var_t, AF.Sqrt,
                                                    bias=eps_t[:], scale=1.0 / D), inc=se)
            R.wait("vector", se)
            R.emit("vector", lambda v: v.reciprocal(istd_t, var_t), inc=ve)

        def emit_ln_apply(dst_ap, src_ap, inc=None, bix=0):
            mean_t = mean_t2[:, bix:bix + 1]
            istd_t = istd_t2[:, bix:bix + 1]
            R.wait("vector", ve)
            R.emit("vector", lambda v, d=dst_ap, a=src_ap: v.tensor_scalar(
                d, a, mean_t, istd_t, ALU.subtract, ALU.mult), inc=inc)

        sc_ropes = ExitStack()
        tabC_k = sc_ropes.enter_context(nc.sbuf_tensor("tabC_k", [128, S], BF16))
        tabS2_k = sc_ropes.enter_context(nc.sbuf_tensor("tabS2_k", [128, S], BF16))

        sc_qT = ExitStack()
        qT_sb = sc_qT.enter_context(nc.sbuf_tensor("qT_sb", [128, NH * T], BF16))
        sc_ropes_q = ExitStack()
        tabC_q = sc_ropes_q.enter_context(nc.sbuf_tensor("tabC_q", [128, T], BF16))
        tabS2_q = sc_ropes_q.enter_context(nc.sbuf_tensor("tabS2_q", [128, T], BF16))

        sc_set = ExitStack()
        iot_id = sc_set.enter_context(nc.sbuf_tensor("iot_id", [128, 128], I32))
        posk_i = sc_set.enter_context(nc.sbuf_tensor("posk_i", [64, S], I32))
        ivf_i = sc_set.enter_context(nc.sbuf_tensor("ivf_i", [64, 1], I32))
        ivf = sc_set.enter_context(nc.sbuf_tensor("ivf", [64, 1], F32))
        frq = sc_set.enter_context(nc.sbuf_tensor("frq", [64, S], F32))
        rrd = sc_set.enter_context(nc.sbuf_tensor("rrd", [64, S], F32))
        msk = sc_set.enter_context(nc.sbuf_tensor("msk", [64, S], F32))
        nni = sc_set.enter_context(nc.sbuf_tensor("nni", [64, S], I32))
        posq_f = sc_set.enter_context(nc.sbuf_tensor("posq_f", [64, T], F32))

        R.emit("sync", lambda sy: sy.dma_start(out=posq_f[:], in_=pos_q), inc=cw)
        posq_loaded = cw.n
        R.emit("gpsimd", lambda g: g.iota(iot_id[:], pattern=[[1, 128]], base=0,
                                          channel_multiplier=-1))
        R.emit("gpsimd", lambda g: g.iota(posk_i[:], pattern=[[1, S]], base=0,
                                          channel_multiplier=0))
        R.emit("gpsimd", lambda g: g.iota(ivf_i[:], pattern=[[1, 1]], base=0,
                                          channel_multiplier=1), inc=gp)
        R.emit("vector", lambda v: v.memset(ones_col[:], 1.0))
        R.emit("vector", lambda v: v.memset(onesf_row[:], 1.0))
        R.emit("vector", lambda v: v.memset(c128_2[0:2, :], -128.0))
        R.emit("vector", lambda v: v.memset(ones_bfr[0:1, :], 1.0))
        R.emit("vector", lambda v: v.memset(eps_t[:], EPS))
        R.wait("vector", gp)
        R.emit("vector", lambda v: v.tensor_scalar(ident[:], iot_id[:], 0, None,
                                                   ALU.is_equal))
        R.emit("vector", lambda v: v.tensor_copy(ivf[:], ivf_i[:]), inc=ve)
        R.wait("scalar", ve)
        R.emit("scalar", lambda s: s.activation(ivf[:], ivf[:], AF.Exp,
                                                scale=-2.0 * math.log(10000.0) / HD),
               inc=se)
        R.wait("vector", se)

        def emit_sin(src_ap, dst_ap, width, phase):
            if phase != 0.0:
                R.emit("vector", lambda v, a=src_ap: v.tensor_scalar(
                    rrd[0:64, 0:width], a, phase, None, ALU.add))
                a0 = rrd[0:64, 0:width]
            else:
                a0 = src_ap
            R.emit("vector", lambda v, a=a0: v.tensor_scalar(
                msk[0:64, 0:width], a, 1.0 / TAU, None, ALU.mult))
            R.emit("vector", lambda v: v.tensor_copy(nni[0:64, 0:width], msk[0:64, 0:width]))
            R.emit("vector", lambda v: v.tensor_copy(msk[0:64, 0:width], nni[0:64, 0:width]))
            R.emit("vector", lambda v, a=a0: v.scalar_tensor_tensor(
                rrd[0:64, 0:width], msk[0:64, 0:width], -TAU, a, ALU.mult, ALU.add))
            R.emit("vector", lambda v: v.tensor_scalar(
                msk[0:64, 0:width], rrd[0:64, 0:width], math.pi, None, ALU.is_gt))
            R.emit("vector", lambda v: v.scalar_tensor_tensor(
                rrd[0:64, 0:width], msk[0:64, 0:width], -TAU, rrd[0:64, 0:width],
                ALU.mult, ALU.add))
            R.emit("vector", lambda v: v.tensor_scalar(
                msk[0:64, 0:width], rrd[0:64, 0:width], -math.pi, None, ALU.is_lt))
            R.emit("vector", lambda v: v.scalar_tensor_tensor(
                rrd[0:64, 0:width], msk[0:64, 0:width], TAU, rrd[0:64, 0:width],
                ALU.mult, ALU.add), inc=ve)
            R.wait("scalar", ve)
            R.emit("scalar", lambda s, d=dst_ap: s.activation(d, rrd[0:64, 0:width],
                                                              AF.Sin), inc=se)
            R.wait("vector", se)

        R.emit("vector", lambda v: v.tensor_scalar(frq[0:64, :], posk_i[:], ivf[:],
                                                   None, ALU.mult))
        emit_sin(frq[0:64, :], tabS2_k[64:128, :], S, 0.0)
        R.emit("vector", lambda v: v.tensor_scalar(tabS2_k[0:64, :], tabS2_k[64:128, :],
                                                   -1.0, None, ALU.mult))
        emit_sin(frq[0:64, :], tabC_k[0:64, :], S, math.pi / 2)
        R.emit("vector", lambda v: v.tensor_copy(tabC_k[64:128, :], tabC_k[0:64, :]))
        R.wait("vector", cw, at=posq_loaded)
        R.emit("vector", lambda v: v.tensor_scalar(frq[0:64, 0:T], posq_f[:], ivf[:],
                                                   None, ALU.mult))
        emit_sin(frq[0:64, 0:T], tabS2_q[64:128, :], T, 0.0)
        R.emit("vector", lambda v: v.tensor_scalar(tabS2_q[0:64, :], tabS2_q[64:128, :],
                                                   -1.0, None, ALU.mult))
        emit_sin(frq[0:64, 0:T], tabC_q[0:64, :], T, math.pi / 2)
        R.emit("vector", lambda v: v.tensor_copy(tabC_q[64:128, :], tabC_q[0:64, :]),
               inc=ve)
        R.replay()
        sc_set.close()


        def emit_rope(dst_ap, ps_ap, tabC, tabS2, col0, width, scr, inc=None):
            xf = scr[:, 0:width]
            xsw = scr[:, width:2 * width]
            tc_ = tabC[:, col0:col0 + width]
            ts_ = tabS2[:, col0:col0 + width]
            R.emit("vector", lambda v: v.tensor_copy(xf, ps_ap[:, 0:width]))
            R.emit("vector", lambda v: v.tensor_copy(xsw[0:64, :], ps_ap[64:128, 0:width]))
            R.emit("vector", lambda v: v.tensor_copy(xsw[64:128, :], ps_ap[0:64, 0:width]))
            R.emit("vector", lambda v: v.tensor_tensor(xsw, xsw, ts_, ALU.mult))
            R.emit("vector", lambda v: v.tensor_tensor(xf, xf, tc_, ALU.mult))
            R.emit("vector", lambda v, d=dst_ap: v.tensor_add(d, xf, xsw), inc=inc)

        def make_tr_engine(banks):
            st = {"j": 0, "pe": [], "se": []}

            def tr_batch(pairs):
                j = st["j"]
                bank = banks[j % 4]
                if j >= 4:
                    R.wait("tensor", se, at=st["se"][j - 4])
                n = len(pairs)
                for i, (s_, d_) in enumerate(pairs):
                    R.emit("tensor", lambda t, s_=s_, i=i, bank=bank: t.transpose(
                        bank[:, i * 128:(i + 1) * 128], s_, ident[:]),
                        inc=pe if i == n - 1 else None)
                st["pe"].append(pe.n)
                R.wait("scalar", pe, at=pe.n)
                for i, (s_, d_) in enumerate(pairs):
                    R.emit("scalar", lambda s, d_=d_, i=i, bank=bank: s.copy(
                        d_, bank[:, i * 128:(i + 1) * 128]),
                        inc=se if i == n - 1 else None)
                st["se"].append(se.n)
                st["j"] += 1
            return tr_batch

        sc_a = ExitStack()
        wq_sb = sc_a.enter_context(nc.sbuf_tensor("wq_sb", [128, DC * D], BF16))
        xnT_own = sc_a.enter_context(nc.sbuf_tensor("xnT_own", [128, DC * T], BF16))
        x_in = sc_a.enter_context(nc.sbuf_tensor("x_in", [128, 3 * D], BF16))
        xn_bf = sc_a.enter_context(nc.sbuf_tensor("xn_bf", [128, 3 * D], BF16))
        rope_scr = sc_a.enter_context(nc.sbuf_tensor("rope_scr", [128, 1024], F32))
        ps_mm0 = sc_a.enter_context(nc.psum_tensor("ps_mm0", [128, 512], F32))
        ps_mm1 = sc_a.enter_context(nc.psum_tensor("ps_mm1", [128, 512], F32))
        ps_trb = [sc_a.enter_context(nc.psum_tensor(f"ps_trb{i}", [128, 1024], BF16))
                  for i in range(4)]
        ps_m2 = sc_a.enter_context(nc.psum_tensor("ps_m2", [128, 512], F32))
        ps_mm = [ps_mm0, ps_mm1]
        tr_eng_a0 = make_tr_engine(ps_trb)

        for b in range(4):
            src = wq_t[b * 512:(b + 1) * 512, :].rearrange("(c p) q -> p c q", p=128)
            R.emit("gpsimd", lambda g, s=src, b=b: g.dma_start(
                out=wq_sb[:, b * 4 * D:(b + 1) * 4 * D].rearrange("p (c q) -> p c q", c=4),
                in_=s), inc=cw)
        wq_loaded = cw.n

        evac_se = []
        a0_ln_ve = []
        a0_ln_se = []
        tr_pe_tile = []

        def a0_emit_ln(tc):
            buf = tc % 3
            if tc >= 3:
                R.wait("gpsimd", ve, at=a0_ln_ve[tc - 3])
                R.wait("gpsimd", se, at=a0_ln_se[tc - 3])
            R.emit("gpsimd", lambda g, tc=tc, buf=buf: g.dma_start(
                out=x_in[:, buf * D:(buf + 1) * D],
                in_=x_own[tc * 128:(tc + 1) * 128, :]), inc=cin[buf])
            R.wait("vector", cin[buf])
            src = x_in[:, buf * D:(buf + 1) * D]
            if tc >= 3:
                R.wait("vector", pe, at=tr_pe_tile[tc - 3])
                R.wait("scalar", pe, at=tr_pe_tile[tc - 3])
            emit_ln_stats(src, bix=tc % 2, sq_dst=xn_bf[:, buf * D:(buf + 1) * D])
            a0_ln_se.append(se.n)
            emit_ln_apply(xn_bf[:, buf * D:(buf + 1) * D], src, inc=ve, bix=tc % 2)
            a0_ln_ve.append(ve.n)

        def a0_emit_tr(tc):
            buf = tc % 3
            R.wait("tensor", ve, at=a0_ln_ve[tc])
            for b4 in range(4):
                pairs = [(xn_bf[:, buf * D + (b4 * 4 + i) * 128:
                                buf * D + (b4 * 4 + i + 1) * 128],
                          xnT_own[:, (b4 * 4 + i) * T + tc * 128:
                                  (b4 * 4 + i) * T + (tc + 1) * 128])
                         for i in range(4)]
                tr_eng_a0(pairs)
            tr_pe_tile.append(pe.n)

        for tc in range(NTC):
            a0_emit_ln(tc)
            if tc >= 1:
                a0_emit_tr(tc - 1)
        a0_emit_tr(NTC - 1)
        a0_done_se = se.n

        if dbg_tensor("xnT_own", [128, DC * T], BF16) is not None:
            R.wait("sync", se, at=a0_done_se)
            R.emit("sync", lambda sy: sy.dma_start(out=dbg_ap["xnT_own"],
                                                   in_=xnT_own[:]), inc=cst)

        q_ve_done = []
        for h in range(NH):
            for qt in range(2):
                idx = h * 2 + qt
                b = idx % 2
                if idx == 0:
                    R.wait("tensor", cw, at=wq_loaded)
                    R.wait("tensor", se, at=a0_done_se)
                if idx >= 2:
                    R.wait("tensor", ve, at=q_ve_done[idx - 2])
                for c in range(DC):
                    R.emit("tensor", lambda t, h=h, qt=qt, c=c, b=b: t.matmul(
                        ps_mm[b][:],
                        wq_sb[:, c * D + h * 128:c * D + (h + 1) * 128],
                        xnT_own[:, c * T + qt * 512:c * T + (qt + 1) * 512],
                        start=(c == 0), stop=(c == DC - 1)),
                        inc=pe if c == DC - 1 else None)
                R.wait("vector", pe)
                emit_rope(qT_sb[:, h * T + qt * 512:h * T + (qt + 1) * 512],
                          ps_mm[b][:], tabC_q, tabS2_q, qt * 512, 512, rope_scr, inc=ve)
                q_ve_done.append(ve.n)
        R.wait("scalar", ve)
        m2_pe = 0
        for h in range(NH):
            if h > 0:
                R.wait("scalar", pe, at=m2_pe)
            R.emit("scalar", lambda s, h=h: s.activation(
                xn_bf[:, 0:1024], qT_sb[:, h * T:(h + 1) * T], AF.Square), inc=se)
            R.wait("tensor", se)
            for half in range(2):
                if h > 0 or half > 0:
                    R.wait("tensor", ve)
                R.emit("tensor", lambda t, half=half: t.matmul(
                    ps_m2[0:1, 0:512], ones_col[:],
                    xn_bf[:, half * 512:(half + 1) * 512],
                    start=True, stop=True), inc=pe)
                m2_pe = pe.n
                R.wait("vector", pe)
                R.emit("vector", lambda v: v.reduce_max(
                    t11[0:1, 0:1], ps_m2[0:1, 0:512], axis=AX.X), inc=ve)
                R.wait("vector", ve)
                if half == 0:
                    R.emit("vector", lambda v, h=h: v.tensor_copy(
                        m2q16[0:1, h:h + 1], t11[0:1, 0:1]), inc=ve)
                else:
                    R.emit("vector", lambda v, h=h: v.tensor_tensor(
                        m2q16[0:1, h:h + 1], m2q16[0:1, h:h + 1], t11[0:1, 0:1],
                        ALU.max), inc=ve)
                R.wait("vector", ve)
        a2_done_ve = ve.n

        if dbg_tensor("qT", [128, NH * T], BF16) is not None:
            R.wait("sync", ve, at=a2_done_ve)
            R.emit("sync", lambda sy: sy.dma_start(out=dbg_ap["qT"], in_=qT_sb[:]), inc=cst)

        R.replay()
        sc_a.close()
        sc_ropes_q.close()

        sc_kv = ExitStack()
        kT_sb = sc_kv.enter_context(nc.sbuf_tensor("kT_sb", [128, NKV * S], BF16))
        v_sb = sc_kv.enter_context(nc.sbuf_tensor("v_sb", [128, (S // 128) * NKV * HD],
                                                  BF16))
        sc_a1 = ExitStack()
        xnT_g = sc_a1.enter_context(nc.sbuf_tensor("xnT_g", [128, 2 * DC * 512], BF16))
        wk_sb = sc_a1.enter_context(nc.sbuf_tensor("wk_sb", [128, DC * 512], BF16))
        wv_sb = sc_a1.enter_context(nc.sbuf_tensor("wv_sb", [128, DC * 512], BF16))
        x_in = sc_a1.enter_context(nc.sbuf_tensor("x_in1", [128, 3 * D], BF16))
        xn_bf = sc_a1.enter_context(nc.sbuf_tensor("xn_bf1", [128, 3 * D], BF16))
        rope_scr = sc_a1.enter_context(nc.sbuf_tensor("rope_scr1", [128, 1024], BF16))
        ps_mm0 = sc_a1.enter_context(nc.psum_tensor("ps1_mm0", [128, 512], F32))
        ps_mm1 = sc_a1.enter_context(nc.psum_tensor("ps1_mm1", [128, 512], F32))
        ps_trb1 = [sc_a1.enter_context(nc.psum_tensor(f"ps1_trb{i}", [128, 1024], BF16))
                   for i in range(4)]
        ps_m2 = sc_a1.enter_context(nc.psum_tensor("ps1_m2", [128, 512], F32))
        ps_mm = [ps_mm0, ps_mm1]
        tr_eng_a1 = make_tr_engine(ps_trb1)

        for b in range(2):
            src = wk_t[b * 1024:(b + 1) * 1024, :].rearrange("(c p) q -> p c q", p=128)
            R.emit("gpsimd", lambda g, s=src, b=b: g.dma_start(
                out=wk_sb[:, b * 8 * 512:(b + 1) * 8 * 512].rearrange("p (c q) -> p c q", c=8),
                in_=s), inc=cw)
        for b in range(2):
            src = wv_t[b * 1024:(b + 1) * 1024, :].rearrange("(c p) q -> p c q", p=128)
            R.emit("gpsimd", lambda g, s=src, b=b: g.dma_start(
                out=wv_sb[:, b * 8 * 512:(b + 1) * 8 * 512].rearrange("p (c q) -> p c q", c=8),
                in_=s), inc=cw)
        wkv_loaded = cw.n

        grp_pe_done = []
        xn_ready = []
        kv_tail = [None]
        a1_ln_ve = []
        a1_ln_se = []
        tr_pe_tile1 = []
        evac_se1 = []

        def emit_kv(g):
            gb = g % 2
            goff = gb * DC * 512
            rope_ve = []
            vevac_ve = []
            for g4 in range(NKV):
                b = g4 % 2
                if g == 0 and g4 == 0:
                    R.wait("tensor", cw, at=wkv_loaded)
                R.wait("tensor", se, at=xn_ready[g])
                if g4 >= 2:
                    R.wait("tensor", ve, at=rope_ve[g4 - 2])
                elif kv_tail[0] is not None:
                    R.wait("tensor", ve, at=kv_tail[0][g4])
                for c in range(DC):
                    R.emit("tensor", lambda t, g4=g4, c=c, b=b, goff=goff: t.matmul(
                        ps_mm[b][:],
                        wk_sb[:, c * 512 + g4 * 128:c * 512 + (g4 + 1) * 128],
                        xnT_g[:, goff + c * 512:goff + (c + 1) * 512],
                        start=(c == 0), stop=(c == DC - 1)),
                        inc=pe if c == DC - 1 else None)
                R.wait("vector", pe)
                emit_rope(kT_sb[:, g4 * S + g * 512:g4 * S + (g + 1) * 512],
                          ps_mm[b][:], tabC_k, tabS2_k, g * 512, 512, rope_scr, inc=ve)
                rope_ve.append(ve.n)
            for tl in range(4):
                kc = g * 4 + tl
                b = tl % 2
                if tl < 2:
                    R.wait("tensor", ve, at=rope_ve[2 + tl])
                else:
                    R.wait("tensor", ve, at=vevac_ve[tl - 2])
                for c in range(DC):
                    R.emit("tensor", lambda t, tl=tl, c=c, b=b, goff=goff: t.matmul(
                        ps_mm[b][:],
                        xnT_g[:, goff + c * 512 + tl * 128:goff + c * 512 + (tl + 1) * 128],
                        wv_sb[:, c * 512:(c + 1) * 512],
                        start=(c == 0), stop=(c == DC - 1)),
                        inc=pe if c == DC - 1 else None)
                R.wait("vector", pe)
                R.emit("vector", lambda v, kc=kc, b=b: v.tensor_copy(
                    v_sb[:, kc * 512:(kc + 1) * 512], ps_mm[b][:]), inc=ve)
                vevac_ve.append(ve.n)
            kv_tail[0] = vevac_ve[2:4]
            grp_pe_done.append(pe.n)

        def a1_emit_ln(tc):
            if True:
                buf = tc % 3
                if tc >= 3:
                    R.wait("gpsimd", ve, at=a1_ln_ve[tc - 3])
                    R.wait("gpsimd", se, at=a1_ln_se[tc - 3])
                R.emit("gpsimd", lambda gg, tc=tc, buf=buf: gg.dma_start(
                    out=x_in[:, buf * D:(buf + 1) * D],
                    in_=x_slab[tc * 128:(tc + 1) * 128, :]), inc=cin[buf])
                R.wait("vector", cin[buf])
                srcx = x_in[:, buf * D:(buf + 1) * D]
                if tc >= 3:
                    R.wait("vector", pe, at=tr_pe_tile1[tc - 3])
                    R.wait("scalar", pe, at=tr_pe_tile1[tc - 3])
                emit_ln_stats(srcx, bix=tc % 2, sq_dst=xn_bf[:, buf * D:(buf + 1) * D])
                a1_ln_se.append(se.n)
                emit_ln_apply(xn_bf[:, buf * D:(buf + 1) * D], srcx, inc=ve, bix=tc % 2)
                a1_ln_ve.append(ve.n)

        def a1_emit_tr(tc):
            if True:
                g = tc // 4
                tl = tc % 4
                gb = g % 2
                goff = gb * DC * 512
                buf = tc % 3
                R.wait("tensor", ve, at=a1_ln_ve[tc])
                if tl == 0 and g >= 2:
                    R.wait("tensor", pe, at=grp_pe_done[g - 2])
                for b4 in range(4):
                    pairs = [(xn_bf[:, buf * D + (b4 * 4 + i) * 128:
                                    buf * D + (b4 * 4 + i + 1) * 128],
                              xnT_g[:, goff + (b4 * 4 + i) * 512 + tl * 128:
                                    goff + (b4 * 4 + i) * 512 + (tl + 1) * 128])
                             for i in range(4)]
                    tr_eng_a1(pairs)
                tr_pe_tile1.append(pe.n)
                if tl == 3:
                    xn_ready.append(se.n)

        for tc in range(32):
            a1_emit_ln(tc)
            if tc >= 2:
                a1_emit_tr(tc - 2)
                if (tc - 2) % 4 == 3:
                    emit_kv((tc - 2) // 4)
        a1_emit_tr(30)
        a1_emit_tr(31)
        emit_kv(7)

        R.wait("scalar", ve)
        m2_pe = 0
        for g4 in range(NKV):
            for sp in range(8):
                if not (g4 == 0 and sp == 0):
                    R.wait("scalar", pe, at=m2_pe)
                R.emit("scalar", lambda s, g4=g4, sp=sp: s.activation(
                    xn_bf[:, 0:512],
                    kT_sb[:, g4 * S + sp * 512:g4 * S + (sp + 1) * 512],
                    AF.Square), inc=se)
                R.wait("tensor", se)
                if not (g4 == 0 and sp == 0):
                    R.wait("tensor", ve)
                R.emit("tensor", lambda t: t.matmul(
                    ps_m2[0:1, 0:512], ones_col[:], xn_bf[:, 0:512],
                    start=True, stop=True), inc=pe)
                m2_pe = pe.n
                R.wait("vector", pe)
                R.emit("vector", lambda v: v.reduce_max(t11[0:1, 0:1], ps_m2[0:1, 0:512],
                                                        axis=AX.X), inc=ve)
                R.wait("vector", ve)
                if sp == 0:
                    R.emit("vector", lambda v, g4=g4: v.tensor_copy(
                        m2k4[0:1, g4:g4 + 1], t11[0:1, 0:1]), inc=ve)
                else:
                    R.emit("vector", lambda v, g4=g4: v.tensor_tensor(
                        m2k4[0:1, g4:g4 + 1], m2k4[0:1, g4:g4 + 1], t11[0:1, 0:1],
                        ALU.max), inc=ve)
        for h in range(NH):
            R.emit("vector", lambda v, h=h: v.tensor_tensor(
                t11[0:1, h:h + 1], m2q16[0:1, h:h + 1],
                m2k4[0:1, h // 4:h // 4 + 1], ALU.mult),
                inc=ve if h == NH - 1 else None)
        R.wait("scalar", ve)
        R.emit("scalar", lambda s: s.activation(t11[0:1, 0:16], t11[0:1, 0:16], AF.Sqrt,
                                                scale=1.0 / HD), inc=se)
        R.wait("vector", se)
        R.emit("vector", lambda v: v.tensor_scalar(t11[0:1, 0:16], t11[0:1, 0:16],
                                                   -1.0, None, ALU.mult), inc=ve)
        R.wait("tensor", ve)
        R.emit("tensor", lambda t: t.matmul(ps_m2[:, 0:16], onesf_row[:],
                                            t11[0:1, 0:16], start=True, stop=True),
               inc=pe)
        R.wait("vector", pe)
        R.emit("vector", lambda v: v.tensor_copy(shift_sb[:], ps_m2[:, 0:16]), inc=ve)
        a1_done_ve = ve.n

        if dbg_tensor("kT", [128, NKV * S], BF16) is not None:
            dbg_tensor("vsb", [128, (S // 128) * NKV * HD], BF16)
            dbg_tensor("shift", [128, NH])
            R.wait("sync", ve, at=a1_done_ve)
            R.emit("sync", lambda sy: sy.dma_start(out=dbg_ap["kT"], in_=kT_sb[:]), inc=cst)
            R.emit("sync", lambda sy: sy.dma_start(out=dbg_ap["vsb"], in_=v_sb[:]), inc=cst)
            R.emit("sync", lambda sy: sy.dma_start(out=dbg_ap["shift"], in_=shift_sb[:]),
                   inc=cst)

        R.replay()
        sc_a1.close()

        sc_b = ExitStack()
        sc_wo = ExitStack()
        wo_sb = sc_wo.enter_context(nc.sbuf_tensor("wo_sb", [128, DC * D], BF16,
                                                   side="right"))
        for b in range(4):
            srcw = wo_t[b * 512:(b + 1) * 512, :].rearrange("(c p) q -> p c q", p=128)
            R.emit("gpsimd", lambda g, s=srcw, b=b: g.dma_start(
                out=wo_sb[:, b * 4 * D:(b + 1) * 4 * D].rearrange("p (c q) -> p c q", c=4),
                in_=s), inc=cw)
        wo_loaded = cw.n
        pT = sc_b.enter_context(nc.sbuf_tensor("pT", [128, 3 * 1024], BF16))
        bc_sb = sc_b.enter_context(nc.sbuf_tensor("bc_sb", [128, 512], F32))
        dinv = sc_b.enter_context(nc.sbuf_tensor("dinv", [1, 512], BF16))
        evac_hist = []
        oT_st = sc_b.enter_context(nc.sbuf_tensor("oT_st", [128, 2 * 512], BF16))
        ps_s0 = sc_b.enter_context(nc.psum_tensor("ps_s0", [128, 1024], F32))
        ps_s1 = sc_b.enter_context(nc.psum_tensor("ps_s1", [128, 1024], F32))
        ps_pv2 = [sc_b.enter_context(nc.psum_tensor(f"ps_pv{i}", [128, 512], F32))
                  for i in range(2)]
        ps_den2 = [sc_b.enter_context(nc.psum_tensor(f"ps_den{i}", [128, 512], F32))
                   for i in range(2)]
        ps_s = [ps_s0, ps_s1]

        R.wait("tensor", ve, at=a1_done_ve)
        R.wait("scalar", ve, at=a1_done_ve)
        NP = 16
        ospill_cst = []
        prev_exp_se = [None]
        prev_cons_pe = [None]
        prev_evac_ve = [None]
        pending_tail = [None]
        for h in range(NH):
            g4 = h // NKV
            for qt in range(2):
                it = h * 2 + qt
                q0 = qt * 512
                sbuf_b = it % 2
                exp_se = []
                cons_pe = []

                ps_pv = ps_pv2[it % 2]
                ps_den = ps_den2[it % 2]
                ps_bc = ps_den

                def b_emit_s(p):
                    b = p % 2
                    if p >= 2:
                        R.wait("tensor", se, at=exp_se[p - 2])
                    elif prev_exp_se[0] is not None:
                        R.wait("tensor", se,
                               at=prev_exp_se[0][NP - 1 - ((NP - 1 - p) % 2)])
                    for half in range(2):
                        kc = 2 * p + half
                        R.emit("tensor", lambda t, g4=g4, kc=kc, b=b, half=half, h=h,
                               q0=q0: t.matmul(
                                   ps_s[b][:, half * 512:(half + 1) * 512],
                                   kT_sb[:, g4 * S + kc * 128:g4 * S + (kc + 1) * 128],
                                   qT_sb[:, h * T + q0:h * T + q0 + 512],
                                   start=True, stop=True),
                               inc=pe if half == 1 else None)
                    R.wait("scalar", pe)
                    if p >= 3:
                        R.wait("scalar", pe, at=cons_pe[p - 3])
                    elif prev_cons_pe[0] is not None:
                        R.wait("scalar", pe,
                               at=prev_cons_pe[0][NP - 1 - ((NP - 1 - p) % 3)])
                    bpt = p % 3
                    R.emit("scalar", lambda s, h=h, b=b, bpt=bpt: s.activation(
                        pT[:, bpt * 1024:(bpt + 1) * 1024], ps_s[b][:], AF.Exp,
                        bias=shift_sb[:, h:h + 1], scale=RSQ), inc=se)
                    exp_se.append(se.n)

                def b_emit_dpv(p, ps_pv=ps_pv, ps_den=ps_den):
                    b = p % 3
                    R.wait("tensor", se, at=exp_se[p])
                    if p == 0 and len(evac_hist) >= 2:
                        R.wait("tensor", ve, at=evac_hist[-2])
                    for half in range(2):
                        kc = 2 * p + half
                        R.emit("tensor", lambda t, b=b, half=half, kc=kc, ps_den=ps_den:
                               t.matmul(
                            ps_den[0:1, 0:512], ones_col[:],
                            pT[:, b * 1024 + half * 512:b * 1024 + (half + 1) * 512],
                            start=(kc == 0), stop=(kc == 31)))
                        R.emit("tensor", lambda t, b=b, half=half, kc=kc, g4=g4,
                               ps_pv=ps_pv: t.matmul(
                            ps_pv[:],
                            v_sb[:, kc * 512 + g4 * 128:kc * 512 + (g4 + 1) * 128],
                            pT[:, b * 1024 + half * 512:b * 1024 + (half + 1) * 512],
                            start=(kc == 0), stop=(kc == 31)),
                            inc=pe if half == 1 else None)
                    cons_pe.append(pe.n)

                def b_emit_tail(h=h, q0=q0, sbuf_b=sbuf_b, it=it,
                                ps_pv=ps_pv, ps_den=ps_den, ps_bc=ps_bc):
                    R.wait("vector", pe)
                    def _dencp(v):
                        with nc.allow_low_precision(reason="softmax denominator broadcast in bf16; tolerance sized for bf16 matmul path"):
                            return v.tensor_copy(dinv[0:1, :], ps_den[0:1, 0:512])
                    R.emit("vector", _dencp, inc=ve)
                    R.wait("tensor", ve)
                    R.emit("tensor", lambda t: t.matmul(ps_bc[:], ones_bfr[0:1, :],
                                                        dinv[0:1, :],
                                                        start=True, stop=True), inc=pe)
                    R.wait("vector", pe)
                    if it >= 2:
                        R.wait("vector", cst, at=ospill_cst[it - 2])
                    R.emit("vector", lambda v: v.reciprocal(bc_sb[:], ps_bc[:]), inc=ve)
                    R.wait("vector", ve)
                    R.emit("vector", lambda v: v.tensor_tensor(
                        oT_st[:, sbuf_b * 512:(sbuf_b + 1) * 512], ps_pv[:], bc_sb[:],
                        ALU.mult), inc=ve)
                    evac_hist.append(ve.n)
                    R.wait("scalar", ve)
                    R.emit("scalar", lambda s: s.dma_start(
                        out=oT_scr[h, :, q0:q0 + 512],
                        in_=oT_st[:, sbuf_b * 512:(sbuf_b + 1) * 512]), inc=cst)
                    ospill_cst.append(cst.n)

                for p in range(NP):
                    b_emit_s(p)
                    if p == 1 and pending_tail[0] is not None:
                        pending_tail[0]()
                        pending_tail[0] = None
                    if p >= 2:
                        b_emit_dpv(p - 2)
                b_emit_dpv(NP - 2)
                b_emit_dpv(NP - 1)
                prev_exp_se[0] = exp_se
                prev_cons_pe[0] = cons_pe
                pending_tail[0] = b_emit_tail
        pending_tail[0]()
        pending_tail[0] = None
        b_done_cst = cst.n
        b_done_ve = ve.n

        if dbg_tensor("oT", [NH, 128, T], BF16) is not None:
            R.wait("sync", cst, at=b_done_cst)
            R.emit("sync", lambda sy: sy.dma_start(out=dbg_ap["oT"], in_=oT_scr), inc=cst)

        R.replay()
        sc_b.close()
        sc_kv.close()
        sc_qT.close()
        sc_ropes.close()

        sc_cde = ExitStack()
        xn2T = sc_cde.enter_context(nc.sbuf_tensor("xn2T", [128, DC * T], BF16))
        sc_c = ExitStack()
        oT_in = sc_c.enter_context(nc.sbuf_tensor("oT_in", [128, 2 * NH * 512], BF16))
        xo_in = sc_c.enter_context(nc.sbuf_tensor("xo_in", [128, D], F32))
        x1_buf = sc_c.enter_context(nc.sbuf_tensor("x1_buf", [128, 2 * D], F32))
        xn2_bf = sc_c.enter_context(nc.sbuf_tensor("xn2_bf", [128, NTC * D], BF16))
        ps_oA = sc_c.enter_context(nc.psum_tensor("ps_oA", [128, 2048], F32))
        ps_oB = sc_c.enter_context(nc.psum_tensor("ps_oB", [128, 2048], F32))
        ps_o = [ps_oA, ps_oB]
        oA_bf = ps_oA[:].bitcast(BF16)
        tr_banks_c = [oA_bf[:, j * 1024:j * 1024 + 512] for j in range(4)]
        tr_eng_c = make_tr_engine(tr_banks_c)

        for qt in range(2):
            srcq = oT_scr[:, :, qt * 512:(qt + 1) * 512].rearrange("h p t -> p h t")
            if qt == 0:
                R.wait("sync", cst, at=b_done_cst)
            R.emit("sync", lambda sy, s=srcq, qt=qt: sy.dma_start(
                out=oT_in[:, qt * NH * 512:(qt + 1) * NH * 512].rearrange(
                    "p (h t) -> p h t", h=NH), in_=s), inc=cot[qt])

        c_add_ve = []
        c_ln2ap_ve = []
        c_ln3_ve = []
        c_spill_cst = []
        for tc in range(NTC):
            b = tc % 2
            buf = tc % 2
            qt = tc // 4
            if tc >= 1:
                R.wait("sync", ve, at=c_add_ve[tc - 1])
            R.emit("sync", lambda sy, tc=tc: sy.dma_start(
                out=xo_in[:], in_=x_own[tc * 128:(tc + 1) * 128, :]), inc=cin[0])
            if tc == 0:
                R.wait("tensor", cw, at=wo_loaded)
            R.wait("tensor", cot[qt])
            if tc >= 2:
                R.wait("tensor", ve, at=c_ln2ap_ve[tc - 2])
            for h in range(NH):
                for jb in range(4):
                    R.emit("tensor", lambda t, h=h, jb=jb, b=b, qt=qt, tc=tc: t.matmul(
                        ps_o[b][:, jb * 512:(jb + 1) * 512],
                        oT_in[:, qt * NH * 512 + h * 512 + (tc % 4) * 128:
                              qt * NH * 512 + h * 512 + (tc % 4) * 128 + 128],
                        wo_sb[:, h * D + jb * 512:h * D + (jb + 1) * 512],
                        start=(h == 0), stop=(h == NH - 1)),
                        inc=pe if (h == NH - 1 and jb == 3) else None)
            R.wait("vector", pe)
            if tc >= 2:
                R.wait("vector", cst, at=c_spill_cst[tc - 2])
                R.wait("scalar", cst, at=c_spill_cst[tc - 2])
                R.wait("scalar", ve, at=c_ln3_ve[tc - 2])
            emit_ln_stats(ps_o[b][:], bix=buf, sq_dst=x1_buf[:, buf * D:(buf + 1) * D])
            emit_ln_apply(x1_buf[:, buf * D:(buf + 1) * D], ps_o[b][:], inc=ve, bix=buf)
            c_ln2ap_ve.append(ve.n)
            R.wait("vector", cin[0])
            R.emit("vector", lambda v, buf=buf: v.tensor_add(
                x1_buf[:, buf * D:(buf + 1) * D], x1_buf[:, buf * D:(buf + 1) * D],
                xo_in[:]), inc=ve)
            c_add_ve.append(ve.n)
            R.wait("scalar", ve)
            R.emit("scalar", lambda s, tc=tc, buf=buf: s.dma_start(
                out=x1_scr[tc * 128:(tc + 1) * 128, :],
                in_=x1_buf[:, buf * D:(buf + 1) * D]), inc=cst)
            c_spill_cst.append(cst.n)
            src2 = x1_buf[:, buf * D:(buf + 1) * D]
            emit_ln_stats(src2, bix=buf, sq_dst=xn2_bf[:, tc * D:(tc + 1) * D])
            emit_ln_apply(xn2_bf[:, tc * D:(tc + 1) * D], src2, inc=ve, bix=buf)
            c_ln3_ve.append(ve.n)
        R.wait("tensor", ve)
        for tc in range(NTC):
            for b4 in range(4):
                pairs = [(xn2_bf[:, tc * D + (b4 * 4 + i) * 128:
                                tc * D + (b4 * 4 + i + 1) * 128],
                          xn2T[:, (b4 * 4 + i) * T + tc * 128:
                               (b4 * 4 + i) * T + (tc + 1) * 128])
                         for i in range(4)]
                tr_eng_c(pairs)
        c_done_se = se.n
        c_x1_cst = cst.n

        if dbg_tensor("xn2T", [128, DC * T], BF16) is not None:
            dbg_tensor("x1", [T, D])
            R.wait("sync", se, at=c_done_se)
            R.wait("sync", cst, at=c_x1_cst)
            R.emit("sync", lambda sy: sy.dma_start(out=dbg_ap["xn2T"], in_=xn2T[:]),
                   inc=cst)
            R.emit("sync", lambda sy: sy.dma_start(out=dbg_ap["x1"], in_=x1_scr), inc=cst)

        R.replay()
        sc_c.close()
        sc_wo.close()

        sc_h = ExitStack()
        h_sb = sc_h.enter_context(nc.sbuf_tensor("h_sb", [128, ICN * T], BF16))
        sc_d = ExitStack()
        gs_sb = sc_d.enter_context(nc.sbuf_tensor("gs_sb", [128, ICN], F32))
        r2_sb = sc_d.enter_context(nc.sbuf_tensor("r2_sb", [2, T], BF16))
        rh_t = sc_d.enter_context(nc.sbuf_tensor("rh_t", [1, T], BF16))
        rl_t = sc_d.enter_context(nc.sbuf_tensor("rl_t", [1, T], BF16))
        rb_t = sc_d.enter_context(nc.sbuf_tensor("rb_t", [1, T], F32))
        us_sb = sc_d.enter_context(nc.sbuf_tensor("us_sb", [128, ICN], F32))
        wg_buf = sc_d.enter_context(nc.sbuf_tensor("wg_buf", [128, 2 * D], BF16))
        wu_buf = sc_d.enter_context(nc.sbuf_tensor("wu_buf", [128, 2 * D], BF16))
        gs_bf = sc_d.enter_context(nc.sbuf_tensor("gs_bf", [128, 2 * 512], BF16))
        us_bf = sc_d.enter_context(nc.sbuf_tensor("us_bf", [128, 2 * 512], BF16))
        ps_g0 = sc_d.enter_context(nc.psum_tensor("ps_g0", [128, 512], F32))
        ps_g1 = sc_d.enter_context(nc.psum_tensor("ps_g1", [128, 512], F32))
        ps_u0 = sc_d.enter_context(nc.psum_tensor("ps_u0", [128, 512], F32))
        ps_u1 = sc_d.enter_context(nc.psum_tensor("ps_u1", [128, 512], F32))
        ps_r = sc_d.enter_context(nc.psum_tensor("ps_r", [128, 512], F32))
        ps_hs = sc_d.enter_context(nc.psum_tensor("ps_hs", [128, 512], F32))
        ps_gq = [ps_g0, ps_g1]
        ps_uq = [ps_u0, ps_u1]

        R.emit("sync", lambda sy: sy.dma_start(out=gs_sb[:], in_=gs_t), inc=cw)
        R.emit("sync", lambda sy: sy.dma_start(out=us_sb[:], in_=us_t), inc=cw)
        dscales_loaded = cw.n

        R.wait("tensor", se, at=c_done_se)
        for qt in range(2):
            if qt == 1:
                R.wait("tensor", ve)
            for c in range(DC):
                R.emit("tensor", lambda t, qt=qt, c=c: t.matmul(
                    ps_r[0:1, 0:512], ones_col[:],
                    xn2T[:, c * T + qt * 512:c * T + (qt + 1) * 512],
                    start=(c == 0), stop=(c == DC - 1)),
                    inc=pe if c == DC - 1 else None)
            R.wait("vector", pe)
            R.emit("vector", lambda v, qt=qt: v.tensor_copy(
                rr_sb[0:1, qt * 512:(qt + 1) * 512], ps_r[0:1, 0:512]), inc=ve)
        R.wait("vector", ve)
        R.emit("vector", lambda v: v.tensor_copy(rh_t[0:1, :], rr_sb[0:1, :]), inc=ve)
        R.wait("vector", ve)
        R.emit("vector", lambda v: v.tensor_copy(rb_t[0:1, :], rh_t[0:1, :]), inc=ve)
        R.wait("vector", ve)
        R.emit("vector", lambda v: v.tensor_sub(rl_t[0:1, :], rr_sb[0:1, :],
                                                rb_t[0:1, :]), inc=ve)
        R.wait("sync", ve)
        R.emit("sync", lambda sy: sy.dma_start(out=r2_sb[0:1, :], in_=rh_t[0:1, :]),
               inc=cw)
        R.emit("sync", lambda sy: sy.dma_start(out=r2_sb[1:2, :], in_=rl_t[0:1, :]),
               inc=cw)
        r2_cw = cw.n
        R.wait("tensor", cw, at=r2_cw)
        r2_ve = ve.n

        silu_se = {}
        umul_ve = {}
        dma_pe_gate = []
        for ic in range(ICN):
            bb = ic % 2
            if ic >= 2:
                R.wait("gpsimd", pe, at=dma_pe_gate[ic - 2])
            R.emit("gpsimd", lambda g, ic=ic, bb=bb: g.dma_start(
                out=wg_buf[:, bb * D:(bb + 1) * D], in_=wg_t4[ic]), inc=cwg[bb])
            R.emit("gpsimd", lambda g, ic=ic, bb=bb: g.dma_start(
                out=wu_buf[:, bb * D:(bb + 1) * D], in_=wu_t4[ic]), inc=cwu[bb])
            R.wait("tensor", cwg[bb])
            R.wait("tensor", cwu[bb])
            if ic == 0:
                R.wait("tensor", ve, at=r2_ve)
            for qt in range(2):
                for mat in range(2):
                    wbuf = wg_buf if mat == 0 else wu_buf
                    psx = ps_gq[qt] if mat == 0 else ps_uq[qt]
                    if ic >= 1:
                        if mat == 0:
                            R.wait("tensor", se, at=silu_se[(ic - 1, qt)])
                        else:
                            R.wait("tensor", ve, at=umul_ve[(ic - 1, qt)])
                    for c in range(DC):
                        R.emit("tensor", lambda t, wbuf=wbuf, psx=psx, bb=bb, c=c, qt=qt:
                               t.matmul(psx[:],
                                        wbuf[:, bb * D + c * 128:bb * D + (c + 1) * 128],
                                        xn2T[:, c * T + qt * 512:c * T + (qt + 1) * 512],
                                        start=(c == 0), stop=False))
                    R.emit("tensor", lambda t, psx=psx, qt=qt: t.matmul(
                        psx[:], c128_2[0:2, :], r2_sb[0:2, qt * 512:(qt + 1) * 512],
                        start=False, stop=True), inc=pe)
                    if mat == 0:
                        R.wait("scalar", pe)
                        if ic == 0 and qt == 0:
                            R.wait("scalar", cw, at=dscales_loaded)
                        if ic >= 2:
                            R.wait("scalar", ve, at=umul_ve[(ic - 2, qt)])
                        R.emit("scalar", lambda s, ic=ic, qt=qt, bb=bb: s.activation(
                            gs_bf[:, bb * 512:(bb + 1) * 512], ps_gq[qt][:], AF.Silu,
                            scale=gs_sb[:, ic:ic + 1]), inc=se)
                        silu_se[(ic, qt)] = se.n
                    else:
                        R.wait("vector", pe)
                        if ic >= 2:
                            R.wait("vector", ve, at=umul_ve[(ic - 2, qt)])
                        R.emit("vector", lambda v, ic=ic, qt=qt, bb=bb: v.tensor_scalar(
                            us_bf[:, bb * 512:(bb + 1) * 512], ps_uq[qt][:],
                            us_sb[:, ic:ic + 1], None, ALU.mult))
                        R.wait("vector", se, at=silu_se[(ic, qt)])
                        R.emit("vector", lambda v, ic=ic, qt=qt, bb=bb: v.tensor_tensor(
                            h_sb[:, ic * T + qt * 512:ic * T + (qt + 1) * 512],
                            gs_bf[:, bb * 512:(bb + 1) * 512],
                            us_bf[:, bb * 512:(bb + 1) * 512], ALU.mult), inc=ve)
                        umul_ve[(ic, qt)] = ve.n
            dma_pe_gate.append(pe.n)
            icd = ic - 1 if ic >= 1 else None
            if icd is not None:
                R.wait("tensor", ve, at=umul_ve[(icd, 1)])
                for tcx in range(NTC):
                    R.emit("tensor", lambda t, icd=icd, tcx=tcx: t.matmul(
                        ps_hs[:, tcx:tcx + 1],
                        h_sb[:, icd * T + tcx * 128:icd * T + (tcx + 1) * 128],
                        ones_col[:], start=(icd == 0 and tcx == 0), stop=False),
                        inc=pe if tcx == NTC - 1 else None)
        R.wait("tensor", ve, at=umul_ve[(ICN - 1, 1)])
        for tcx in range(NTC):
            R.emit("tensor", lambda t, tcx=tcx: t.matmul(
                ps_hs[:, tcx:tcx + 1],
                h_sb[:, (ICN - 1) * T + tcx * 128:(ICN - 1) * T + tcx * 128 + 128],
                ones_col[:], start=False, stop=True),
                inc=pe if tcx == NTC - 1 else None)
        R.wait("vector", pe)
        R.emit("vector", lambda v: v.tensor_scalar(hs128[:, 0:NTC], ps_hs[:, 0:NTC],
                                                   -128.0, None, ALU.mult), inc=ve)
        d_done_ve = ve.n

        if dbg_tensor("hsb", [128, 4 * T], BF16) is not None:
            R.wait("sync", ve, at=d_done_ve)
            R.emit("sync", lambda sy: sy.dma_start(out=dbg_ap["hsb"],
                                                   in_=h_sb[:, 0:4 * T]), inc=cst)
            dbg_tensor("hs128", [128, NTC])
            R.emit("sync", lambda sy: sy.dma_start(out=dbg_ap["hs128"], in_=hs128[:]),
                   inc=cst)

        R.replay()
        sc_d.close()

        sc_e = ExitStack()
        ds_sb = sc_e.enter_context(nc.sbuf_tensor("ds_sb", [128, D], F32))
        wd_buf = sc_e.enter_context(nc.sbuf_tensor("wd_buf", [128, 2 * 4 * 512], BF16))
        yo_buf = sc_e.enter_context(nc.sbuf_tensor("yo_buf", [128, 4 * 512], F32))
        ps_y = [sc_e.enter_context(nc.psum_tensor(f"ps_y{i}", [128, 512], F32))
                for i in range(NTC)]

        R.emit("sync", lambda sy: sy.dma_start(out=ds_sb[:], in_=ds_b), inc=cw)
        ds_loaded = cw.n
        R.wait("tensor", ve, at=d_done_ve)

        e_evac_ve = {}
        e_spill_cst = {}
        e_mm_pe = {}
        for jq in range(4):
            for icc in range(16):
                bb = icc % 2
                if jq * 16 + icc >= 2:
                    pj, pi = divmod(jq * 16 + icc - 2, 16)
                    R.wait("gpsimd", pe, at=e_dma_gate[(pj, pi)])
                src = wd_t[icc * 4:(icc + 1) * 4, :, jq * 512:(jq + 1) * 512].rearrange(
                    "a p f -> p a f")
                R.emit("gpsimd", lambda g, s=src, bb=bb: g.dma_start(
                    out=wd_buf[:, bb * 4 * 512:(bb + 1) * 4 * 512].rearrange(
                        "p (a f) -> p a f", a=4), in_=s), inc=cwd[bb])
                R.wait("tensor", cwd[bb])
                for l in range(4):
                    ic = icc * 4 + l
                    for tc in range(NTC):
                        if ic == 0 and jq > 0:
                            R.wait("tensor", ve, at=e_evac_ve[(jq - 1, tc)])
                        R.emit("tensor", lambda t, ic=ic, tc=tc, bb=bb, l=l: t.matmul(
                            ps_y[tc][:],
                            h_sb[:, ic * T + tc * 128:ic * T + (tc + 1) * 128],
                            wd_buf[:, (bb * 4 + l) * 512:(bb * 4 + l + 1) * 512],
                            start=(ic == 0), stop=(ic == ICN - 1)),
                            inc=pe if (ic == ICN - 1 or (tc == NTC - 1 and l == 3))
                            else None)
                        if ic == ICN - 1:
                            e_mm_pe[(jq, tc)] = pe.n
                if icc == 0 and jq == 0:
                    e_dma_gate = {}
                e_dma_gate[(jq, icc)] = pe.n
            for tc in range(NTC):
                b4 = tc % 4
                R.wait("vector", pe, at=e_mm_pe[(jq, tc)])
                if jq == 0 and tc == 0:
                    R.wait("vector", cw, at=ds_loaded)
                if (jq, tc) != (0, 0):
                    prev = (jq, tc - 4) if tc >= 4 else (jq - 1, tc + 4)
                    if prev in e_spill_cst:
                        R.wait("vector", cst, at=e_spill_cst[prev])
                R.emit("vector", lambda v, tc=tc, jq=jq, b4=b4: v.scalar_tensor_tensor(
                    yo_buf[:, b4 * 512:(b4 + 1) * 512], ps_y[tc][:],
                    hs128[:, tc:tc + 1], ds_sb[:, jq * 512:(jq + 1) * 512],
                    ALU.add, ALU.mult), inc=ve)
                e_evac_ve[(jq, tc)] = ve.n
                R.wait("scalar", ve)
                R.emit("scalar", lambda s, tc=tc, jq=jq, b4=b4: s.dma_start(
                    out=y_scr[tc * 128:(tc + 1) * 128, jq * 512:(jq + 1) * 512],
                    in_=yo_buf[:, b4 * 512:(b4 + 1) * 512]), inc=cst)
                e_spill_cst[(jq, tc)] = cst.n

        R.replay()
        sc_e.close()
        sc_h.close()
        sc_cde.close()

        sc_f = ExitStack()
        fy = sc_f.enter_context(nc.sbuf_tensor("fy", [128, 2 * D], F32))
        fx = sc_f.enter_context(nc.sbuf_tensor("fx", [128, 2 * D], F32))
        fo = sc_f.enter_context(nc.sbuf_tensor("fo", [128, 2 * D], F32))

        f_ve = []
        for tc in range(NTC):
            buf = tc % 2
            R.wait("sync", cst, at=e_spill_cst[(3, tc)])
            R.wait("sync", cst, at=c_x1_cst)
            if tc >= 2:
                R.wait("sync", ve, at=f_ve[tc - 2])
            R.emit("sync", lambda sy, tc=tc, buf=buf: sy.dma_start(
                out=fy[:, buf * D:(buf + 1) * D],
                in_=y_scr[tc * 128:(tc + 1) * 128, :]), inc=cfy[buf])
            R.emit("sync", lambda sy, tc=tc, buf=buf: sy.dma_start(
                out=fx[:, buf * D:(buf + 1) * D],
                in_=x1_scr[tc * 128:(tc + 1) * 128, :]), inc=cfx[buf])
            R.wait("vector", cfy[buf])
            R.wait("scalar", cfy[buf])
            src = fy[:, buf * D:(buf + 1) * D]
            if tc >= 2:
                R.wait("vector", cst, at=f_out_cst[tc - 2])
                R.wait("scalar", cst, at=f_out_cst[tc - 2])
            emit_ln_stats(src, bix=buf, sq_dst=fo[:, buf * D:(buf + 1) * D])
            emit_ln_apply(fo[:, buf * D:(buf + 1) * D], src, bix=buf)
            R.wait("vector", cfx[buf])
            R.emit("vector", lambda v, buf=buf: v.tensor_add(
                fo[:, buf * D:(buf + 1) * D], fo[:, buf * D:(buf + 1) * D],
                fx[:, buf * D:(buf + 1) * D]), inc=ve)
            f_ve.append(ve.n)
            R.wait("scalar", ve)
            R.emit("scalar", lambda s, tc=tc, buf=buf: s.dma_start(
                out=out[tc * 128:(tc + 1) * 128, :],
                in_=fo[:, buf * D:(buf + 1) * D]), inc=cst)
            if tc == 0:
                f_out_cst = []
            f_out_cst.append(cst.n)

        R.wait("sync", cst)
        R.replay()
        sc_f.close()

    return nc, dbg_ap


def make_in_maps(inputs):
    x = np.asarray(inputs["L0_attn_hidden_states"], dtype=np.float32)
    wq = np.asarray(inputs["L0_attn_Wq"], dtype=np.float32)
    wk = np.asarray(inputs["L0_attn_Wk"], dtype=np.float32)
    wv = np.asarray(inputs["L0_attn_Wv"], dtype=np.float32)
    wo = np.asarray(inputs["L0_attn_Wo"], dtype=np.float32)
    gw = np.asarray(inputs["L0_ffn_gate_w"], dtype=np.int32)
    uw = np.asarray(inputs["L0_ffn_up_w"], dtype=np.int32)
    dw = np.asarray(inputs["L0_ffn_down_w"], dtype=np.int32)
    gsc = np.asarray(inputs["L0_ffn_gate_scale"], dtype=np.float32)
    usc = np.asarray(inputs["L0_ffn_up_scale"], dtype=np.float32)
    dsc = np.asarray(inputs["L0_ffn_down_scale"], dtype=np.float32)

    wq_t = np.ascontiguousarray(wq.T)
    wk_t = np.ascontiguousarray(wk.T)
    wv_t = np.ascontiguousarray(wv.T)
    wo_t = np.ascontiguousarray(wo.T)
    wg_t4 = np.ascontiguousarray(gw.reshape(ICN, 128, DC, 128).transpose(0, 3, 2, 1))
    wu_t4 = np.ascontiguousarray(uw.reshape(ICN, 128, DC, 128).transpose(0, 3, 2, 1))
    wd_t = np.ascontiguousarray(dw.T.reshape(ICN, 128, D))
    gs_t = np.ascontiguousarray(gsc.reshape(ICN, 128).T)
    us_t = np.ascontiguousarray(usc.reshape(ICN, 128).T)
    ds_bh = np.ascontiguousarray(np.broadcast_to(dsc[None, :], (128, D)))

    in_maps = []
    for core in range(8):
        bi = core // 4
        r0 = (core % 4) * T
        posq = np.ascontiguousarray(
            np.broadcast_to(np.arange(r0, r0 + T, dtype=np.float32)[None, :], (64, T)))
        in_maps.append({
            "x_slab": np.ascontiguousarray(x[bi]),
            "x_own": np.ascontiguousarray(x[bi, r0:r0 + T]),
            "pos_q": posq,
            "wq_t": wq_t, "wk_t": wk_t, "wv_t": wv_t, "wo_t": wo_t,
            "wg_t4": wg_t4, "wu_t4": wu_t4, "wd_t": wd_t,
            "gs_t": gs_t, "us_t": us_t, "ds_b": ds_bh,
        })
    return in_maps


def kernel(**inputs):
    from concourse.bass_utils import run_bass_kernel_spmd
    nc, _ = build_nc()
    in_maps = make_in_maps(inputs)
    res = run_bass_kernel_spmd(nc, in_maps, core_ids=list(range(8)))
    outp = np.empty((2, S, D), dtype=np.float32)
    for core in range(8):
        bi = core // 4
        r0 = (core % 4) * T
        outp[bi, r0:r0 + T] = res.results[core]["out"]
    return outp
